# version 11
# baseline (speedup 1.0000x reference)
import numpy as np
import ml_dtypes
from contextlib import ExitStack

import concourse.bass as bass
import concourse.mybir as mybir
from concourse.bass_utils import run_bass_kernel_spmd

F32 = mybir.dt.float32
BF16 = mybir.dt.bfloat16
I32 = mybir.dt.int32
AF = mybir.ActivationFunctionType
ALU = mybir.AluOpType
AX = mybir.AxisListType

T = 2048
S = 4096
D = 1024
NH = 16
HD = 64
NKV = 4
I0 = 2816
I1 = 3072
TT = T // 128
ST = S // 128
DC = D // 128
L0_SHIFT = 32.0
BF = ml_dtypes.bfloat16


def alibi_slopes():
    x = (2.0 ** 8) ** (1.0 / NH)
    return np.asarray([1.0 / x ** (i + 1) for i in range(NH)], dtype=np.float64)


def make_alibi_table():
    sl = alibi_slopes()
    tbl = np.zeros((128, NH * ST), dtype=np.float64)
    p = np.arange(128)
    for h in range(NH):
        M = sl[h] * (S - 1) + 8.0
        for c in range(ST):
            tbl[:, h * ST + c] = sl[h] * (128 * c + p) - M
    return np.ascontiguousarray(tbl.astype(np.float32))


class Ctx:

    def __init__(self, nc, stack):
        self.nc = nc
        self.stack = stack
        self.sems = {}
        self.cnt = {}

    def sem(self, name):
        if name not in self.sems:
            self.sems[name] = self.stack.enter_context(self.nc.semaphore(name))
            self.cnt[name] = 0
        return self.sems[name]

    def inc(self, instr, name, k=1):
        instr.then_inc(self.sem(name), k)
        self.cnt[name] += k

    def wait(self, eng, name, val=None):
        eng.wait_ge(self.sem(name), self.cnt[name] if val is None else val)

    def sb(self, name, shape, dt=F32):
        return self.stack.enter_context(self.nc.sbuf_tensor(name, shape, dt))

    def ps(self, name, shape, dt=F32):
        return self.stack.enter_context(self.nc.psum_tensor(name, shape, dt))


def mk_quant_scratch(ctx, tag, n):
    return {"SC1": ctx.sb(f"{tag}_SC1", [128, n]),
            "SC2": ctx.sb(f"{tag}_SC2", [128, n]),
            "SCI": ctx.sb(f"{tag}_SCI", [128, n], I32),
            "SGN": ctx.sb(f"{tag}_SGN", [128, 2 * n], BF16),
            "BM": ctx.sb(f"{tag}_BM", [128, n // 16]),
            "SCL": ctx.sb(f"{tag}_SCL", [128, n // 16]),
            "INV": ctx.sb(f"{tag}_INV", [128, n // 16])}


def emit_fp4_dve(v, sc, src, dst, n, sgn_buf=0):
    nb = n // 16
    src3 = src.rearrange("p (b w) -> p b w", w=16)
    BM, SCL, INV = sc["BM"][:, 0:nb], sc["SCL"][:, 0:nb], sc["INV"][:, 0:nb]
    SC1, SC2 = sc["SC1"][:, 0:n], sc["SC2"][:, 0:n]
    nsg = sc["SGN"].ap().shape[1] // 2
    SCI = sc["SCI"][:, 0:n]
    SGN = sc["SGN"][:, sgn_buf * nsg:sgn_buf * nsg + n]
    sc1_3 = SC1.rearrange("p (b w) -> p b w", w=16)
    sc2_3 = SC2.rearrange("p (b w) -> p b w", w=16)
    v.drain()
    v.tensor_reduce(out=BM, in_=src3, axis=AX.X, op=ALU.max,
                    apply_absolute_value=True)
    v.drain()
    v.tensor_scalar(out=SCL, in0=BM, scalar1=1.0 / 6.0, scalar2=1e-12,
                    op0=ALU.mult, op1=ALU.max)
    v.drain()
    v.reciprocal(out=INV, in_=SCL)
    v.drain()
    invb = INV.unsqueeze(2).broadcast_to([128, nb, 16])
    sclb = SCL.unsqueeze(2).broadcast_to([128, nb, 16])
    v.tensor_tensor(out=sc1_3, in0=src3, in1=invb, op=ALU.mult)
    v.scalar_tensor_tensor(out=SC2, in0=SC1, scalar=-1.0, in1=SC1,
                           op0=ALU.mult, op1=ALU.max)
    v.tensor_copy(out=SCI, in_=SC2)
    v.tensor_scalar(out=SC1, in0=SCI, scalar1=-2.0, scalar2=0.0,
                    op0=ALU.add, op1=ALU.max)
    v.tensor_scalar(out=SC1, in0=SC1, scalar1=2.0, scalar2=None,
                    op0=ALU.min)
    v.tensor_scalar(out=SCI, in0=SC2, scalar1=2.0, scalar2=None,
                    op0=ALU.mult)
    v.tensor_scalar(out=SCI, in0=SCI, scalar1=4, scalar2=None, op0=ALU.min)
    v.scalar_tensor_tensor(out=SC1, in0=SCI, scalar=0.5, in1=SC1,
                           op0=ALU.mult, op1=ALU.add)
    v.tensor_scalar(out=SCI, in0=SC2, scalar1=0.5, scalar2=None,
                    op0=ALU.mult)
    v.tensor_scalar(out=SC2, in0=SCI, scalar1=-2.0, scalar2=0.0,
                    op0=ALU.add, op1=ALU.max)
    v.scalar_tensor_tensor(out=SC2, in0=SC2, scalar=2.0, in1=SC1,
                           op0=ALU.mult, op1=ALU.add)
    v.tensor_tensor(out=sc2_3, in0=sc2_3, in1=sclb, op=ALU.mult)
    return v.tensor_tensor(out=dst, in0=SC2, in1=SGN, op=ALU.mult)


def np_fp4_quant(x):
    x = np.asarray(x, dtype=np.float64)
    shp = x.shape
    xb = x.reshape(-1, 16)
    scl = np.maximum(np.abs(xb).max(axis=1, keepdims=True) / 6.0, 1e-12)
    inv = (1.0 / scl.astype(np.float32)).astype(np.float64)
    a = np.abs(xb * inv)
    q = (np.minimum(np.rint(2 * a), 4) * 0.5
         + np.minimum(np.maximum(np.rint(a) - 2, 0), 2)
         + 2 * np.maximum(np.rint(a / 2) - 2, 0))
    return (q * scl * np.sign(xb)).reshape(shp)


def emit_ln_phase(ctx, block, x_src, xnT, xqT, own_off, tag):
    quant = xqT is not None
    XT = [ctx.sb(f"{tag}_XT{i}", [128, D]) for i in range(TT)]
    STATS = ctx.sb(f"{tag}_STATS", [128, 4 * TT])
    XN = [ctx.sb(f"{tag}_XN{i}", [128, D]) for i in range(2)]
    XNB = [ctx.sb(f"{tag}_XNB{i}", [128, D], BF16) for i in range(2)]
    SQT = ctx.sb(f"{tag}_SQT", [128, D])
    sc = mk_quant_scratch(ctx, tag + "_q", D) if quant else None
    XQB = [ctx.sb(f"{tag}_XQB{i}", [128, D], BF16) for i in range(2)] if quant else None

    ld, vsem, asem, tp = (f"{tag}_ld", f"{tag}_v", f"{tag}_a", f"{tag}_tp")
    tpq = f"{tag}_tpq"
    ctx.sem(tp)
    ctx.sem(tpq)
    tp_after = {}
    asem_after = {}

    @block.sync
    def _(sync):
        for tt in range(TT):
            ctx.inc(sync.dma_start(out=XT[tt][:, :],
                                   in_=x_src[tt * 128:(tt + 1) * 128, :]), ld, 16)

    @block.vector
    def _(v):
        ctx.wait(v, ld)
        for tt in range(TT):
            v.tensor_reduce(out=STATS[:, tt:tt + 1], in_=XT[tt][:, :],
                            axis=AX.X, op=ALU.add)
            v.tensor_tensor(out=SQT[:, :], in0=XT[tt][:, :], in1=XT[tt][:, :],
                            op=ALU.mult)
            v.tensor_reduce(out=STATS[:, TT + tt:TT + tt + 1], in_=SQT[:, :],
                            axis=AX.X, op=ALU.add)
        sums, sqs = STATS[:, 0:TT], STATS[:, TT:2 * TT]
        mean = STATS[:, 2 * TT:3 * TT]
        v.drain()
        v.tensor_scalar(out=mean, in0=sums, scalar1=1.0 / D, scalar2=None,
                        op0=ALU.mult)
        v.tensor_scalar(out=sqs, in0=sqs, scalar1=1.0 / D, scalar2=None,
                        op0=ALU.mult)
        v.drain()
        v.tensor_tensor(out=sums, in0=mean, in1=mean, op=ALU.mult)
        v.drain()
        v.tensor_tensor(out=sqs, in0=sqs, in1=sums, op=ALU.subtract)
        v.drain()
        v.tensor_scalar(out=sqs, in0=sqs, scalar1=1e-5, scalar2=None,
                        op0=ALU.add)
        v.drain()
        ctx.inc(v.reciprocal(out=sums, in_=sqs), vsem)

    @block.scalar
    def _(a):
        ctx.wait(a, vsem)
        ctx.inc(a.activation(out=STATS[:, 3 * TT:4 * TT], in_=STATS[:, 0:TT],
                             func=AF.Sqrt), asem)

    for tt in range(TT):
        buf = tt % 2

        @block.vector
        def _(v, tt=tt, buf=buf):
            if tt == 0:
                ctx.wait(v, asem)
            if tt >= 2:
                ctx.wait(v, tp, tp_after[tt - 2][0])
                if quant:
                    ctx.wait(v, tpq, tp_after[tt - 2][1])
                    ctx.wait(v, asem, asem_after[tt - 2])
            mean = STATS[:, 2 * TT + tt:2 * TT + tt + 1]
            rstd = STATS[:, 3 * TT + tt:3 * TT + tt + 1]
            v.tensor_scalar(out=XN[buf][:, :], in0=XT[tt][:, :], scalar1=mean,
                            scalar2=rstd, op0=ALU.subtract, op1=ALU.mult)
            ctx.inc(v.tensor_copy(out=XNB[buf][:, :], in_=XN[buf][:, :]), vsem)

        if quant:
            @block.scalar
            def _(a, buf=buf):
                ctx.wait(a, vsem)
                ctx.inc(a.sign(out=sc["SGN"][:, 0:D], in_=XN[buf][:, :]), asem)
            asem_after[tt] = ctx.cnt[asem]

            @block.vector
            def _(v, buf=buf):
                ctx.wait(v, asem)
                last = emit_fp4_dve(v, sc, XN[buf][:, :], XQB[buf][:, :], D)
                ctx.inc(last, vsem)

        @block.sync
        def _(sync, tt=tt, buf=buf):
            ctx.wait(sync, vsem)
            for c in range(DC):
                ctx.inc(sync.dma_start_transpose(
                    out=xnT[c][:, own_off + tt * 128: own_off + (tt + 1) * 128],
                    in_=XNB[buf][:, c * 128:(c + 1) * 128]), tp, 16)

        if quant:
            @block.sync
            def _(sync, tt=tt, buf=buf):
                ctx.wait(sync, vsem)
                for c in range(DC):
                    ctx.inc(sync.dma_start_transpose(
                        out=xqT[c][:, tt * 128:(tt + 1) * 128],
                        in_=XQB[buf][:, c * 128:(c + 1) * 128]), tpq, 16)
        tp_after[tt] = (ctx.cnt[tp], ctx.cnt[tpq])

    return tp


def build_launch1():
    nc = bass.Bass("TRN2", target_bir_lowering=False)
    x_in = nc.dram_tensor("x_in", [T, D], F32, kind="ExternalInput").ap()
    xnT_out = nc.dram_tensor("xnT_own", [D, T], BF16, kind="ExternalOutput").ap()
    xqT_out = nc.dram_tensor("xqT_own", [D, T], BF16, kind="ExternalOutput").ap()
    with ExitStack() as stack:
        ctx = Ctx(nc, stack)
        block = stack.enter_context(nc.Block())
        xnT = [ctx.sb(f"xnT{c}", [128, T], BF16) for c in range(DC)]
        xqT = [ctx.sb(f"xqT{c}", [128, T], BF16) for c in range(DC)]
        tp = emit_ln_phase(ctx, block, x_in, xnT, xqT, 0, "ln0")

        @block.gpsimd
        def _(g):
            ctx.wait(g, tp)
            ctx.wait(g, "ln0_tpq")
            for c in range(DC):
                ctx.inc(g.dma_start(out=xnT_out[c * 128:(c + 1) * 128, :],
                                    in_=xnT[c][:, :]), "st", 16)
                ctx.inc(g.dma_start(out=xqT_out[c * 128:(c + 1) * 128, :],
                                    in_=xqT[c][:, :]), "st", 16)
            ctx.wait(g, "st")
    return nc


def shard_inputs(full):
    x = np.ascontiguousarray(np.asarray(full["L0_attn_x"]))
    shards = []
    for c in range(8):
        bi, hf = c // 2, c % 2
        shards.append(np.ascontiguousarray(x[bi, hf * T:(hf + 1) * T, :]))
    return shards


def emit_barrier(ctx, block, engines=("sync", "vector", "scalar", "tensor",
                                      "gpsimd"), exclude=()):
    snap = {k: v for k, v in ctx.cnt.items() if v > 0 and k not in exclude}
    for engname in engines:
        dec = getattr(block, engname)

        @dec
        def _(e, snap=snap):
            for name, val in snap.items():
                e.wait_ge(ctx.sem(name), val)


QW = 1024
PAIRS = [[0, 1], [2, 3], [4, 5], [6, 7]]


def sub_ctx(ctx, scope):
    c = Ctx(ctx.nc, scope)
    c.sems, c.cnt = ctx.sems, ctx.cnt
    return c


def emit_wquant(ctx, block, raw_ap, R, C, dstT, sc, tag, dt_raw=F32):
    with ExitStack() as _wq_scope:
        _c = sub_ctx(ctx, _wq_scope)
        _emit_wquant(_c, block, raw_ap, R, C, dstT, sc, tag, dt_raw)


def _emit_wquant(ctx, block, raw_ap, R, C, dstT, sc, tag, dt_raw=F32):
    RC_, CC = R // 128, C // 128
    cchunks = [(c0, min(QW, C - c0)) for c0 in range(0, C, QW)]
    NCH = len(cchunks)
    WRAW = [ctx.sb(f"{tag}_WRAW{i}", [128, C], dt_raw) for i in range(2)]
    WQB = [ctx.sb(f"{tag}_WQB{i}", [128, C], BF16) for i in range(2)]
    ld, vs, as_, tp = f"{tag}_ld", f"{tag}_v", f"{tag}_a", f"{tag}_tp"
    tp2 = f"{tag}_tp2"
    tp_after, a_after, v_after = {}, {}, {}
    for ch in range(RC_):
        buf = ch % 2

        @block.sync
        def _(sync, ch=ch, buf=buf):
            if ch >= 2:
                ctx.wait(sync, vs, v_after[ch - 2])
                ctx.wait(sync, as_, a_after[ch - 2])
            ctx.inc(sync.dma_start(out=WRAW[buf][:, :],
                                   in_=raw_ap[ch * 128:(ch + 1) * 128, :]), ld, 16)

        for ci, (c0, w) in enumerate(cchunks):
            gi = ch * NCH + ci

            @block.scalar
            def _(a, ch=ch, buf=buf, c0=c0, w=w, gi=gi):
                ctx.wait(a, ld)
                if gi >= 2:
                    g2 = gi - 2
                    ctx.wait(a, vs, v_after.get(("c", g2), 0))
                ctx.inc(a.sign(out=sc["SGN"][:, (gi % 2) * QW:(gi % 2) * QW + w],
                               in_=WRAW[buf][:, c0:c0 + w]), as_)
            a_after[ch] = ctx.cnt[as_]

            @block.vector
            def _(v, ch=ch, buf=buf, c0=c0, w=w, gi=gi):
                ctx.wait(v, as_)
                if ch >= 2 and c0 == 0:
                    ctx.wait(v, tp, tp_after[ch - 2][0])
                    if tp_after[ch - 2][1]:
                        ctx.wait(v, tp2, tp_after[ch - 2][1])
                ctx.inc(emit_fp4_dve(v, sc, WRAW[buf][:, c0:c0 + w],
                                     WQB[buf][:, c0:c0 + w], w,
                                     sgn_buf=gi % 2), vs)
            v_after[("c", gi)] = ctx.cnt[vs]
        v_after[ch] = ctx.cnt[vs]

        @block.sync
        def _(sync, ch=ch, buf=buf):
            ctx.wait(sync, vs)
            for j in range(CC):
                ctx.inc(sync.dma_start_transpose(
                    out=dstT[j][:, ch * 128:(ch + 1) * 128],
                    in_=WQB[buf][:, j * 128:(j + 1) * 128]), tp, 16)
        tp_after[ch] = (ctx.cnt[tp], ctx.cnt.get(tp2, 0))

    ctx.sem(tp)
    ctx.sem(tp2)
    return tp


def build_launch2(debug=False):
    nc = bass.Bass("TRN2", target_bir_lowering=False)
    io = {}
    for name, shape, dt, kind in [
        ("x_in", [T, D], F32, "ExternalInput"),
        ("xnT_full", [D, S], BF16, "ExternalInput"),
        ("xnT_own", [D, T], BF16, "ExternalInput"),
        ("xqT_own", [D, T], BF16, "ExternalInput"),
        ("w_gate", [I0, D], F32, "ExternalInput"),
        ("w_up", [I0, D], F32, "ExternalInput"),
        ("w_down", [D, I0], F32, "ExternalInput"),
        ("wv_wT", [D, D], F32, "ExternalInput"),
        ("wv_b", [1, D], F32, "ExternalInput"),
        ("out_wT", [D, D], F32, "ExternalInput"),
        ("out_b", [1, D], F32, "ExternalInput"),
        ("watt", [NH, HD, HD], F32, "ExternalInput"),
        ("eye", [128, 128], F32, "ExternalInput"),
        ("x1_out", [T, D], F32, "ExternalOutput"),
        ("xn1T_out", [D, T], BF16, "ExternalOutput"),
        ("h_park", [T, I0], BF16, "Internal"),
        ("yffn_park", [T, D], BF16, "Internal"),
    ] + ([("o_yffn", [T, D], BF16, "ExternalOutput"),
          ("o_h", [T, I0], BF16, "ExternalOutput")] if debug else []):
        io[name] = nc.dram_tensor(name, shape, dt, kind=kind).ap()
    io["xnT_loads"] = lambda c: [(0, S, io["xnT_full"][c * 128:(c + 1) * 128, :])]
    io["dbg_park_h"] = debug
    with ExitStack() as stack:
        ctx = Ctx(nc, stack)
        block = stack.enter_context(nc.Block())
        emit_l0_body(ctx, stack, block, io)
        if debug:
            @block.gpsimd
            def _(g):
                ctx.wait(g, "stf")
                ctx.inc(g.dma_start(out=io["o_yffn"], in_=io["yffn_park"]),
                        "dbg", 16)
                ctx.inc(g.dma_start(out=io["o_h"], in_=io["h_park"]), "dbg", 16)
                ctx.wait(g, "dbg")
    return nc


def emit_l0_ffn(ctx, block, io):
    nc = ctx.nc
    NHALF = I0 // 2
    NPARTS = [(0, 512), (512, 512), (1024, 384)]
    IC0 = I0 // 128

    with ExitStack() as fs:
        cf = sub_ctx(ctx, fs)
        sc = mk_quant_scratch(cf, "q", QW)
        WGQ_T = [cf.sb(f"WGQT{i}", [128, I0], BF16) for i in range(DC)]
        WUQ_T = [cf.sb(f"WUQT{i}", [128, I0], BF16) for i in range(DC)]
        WDQ_T = [cf.sb(f"WDQT{i}", [128, D], BF16) for i in range(IC0)]
        if "wg_own" in io:
            HI = I0 // 2
            with ExitStack() as ws:
                cw = sub_ctx(ctx, ws)
                TMPg = [cw.sb(f"WTMPg{i}", [128, HI], BF16) for i in range(DC)]
                TMPu = [cw.sb(f"WTMPu{i}", [128, HI], BF16) for i in range(DC)]
                emit_wquant(cw, block, io["wg_own"], HI, D, TMPg, sc, "wg")

                @block.gpsimd
                def _(g):
                    ctx.wait(g, "wg_tp")
                    for c in range(DC):
                        ctx.inc(g.dma_start(
                            out=io["agwg_in"][c * 128:(c + 1) * 128, :],
                            in_=TMPg[c][:, :]), "wgsh", 16)
                    ctx.wait(g, "wgsh")
                    ctx.inc(g.collective_compute(
                        "AllGather", ALU.bypass, replica_groups=PAIRS,
                        ins=[io["agwg_in"]], outs=[io["agwg_out"]]), "ccwg")

                emit_wquant(cw, block, io["wu_own"], HI, D, TMPu, sc, "wu")

                @block.gpsimd
                def _(g):
                    ctx.wait(g, "wu_tp")
                    for c in range(DC):
                        ctx.inc(g.dma_start(
                            out=io["agwu_in"][c * 128:(c + 1) * 128, :],
                            in_=TMPu[c][:, :]), "wush", 16)
                    ctx.wait(g, "wush")
                    ctx.inc(g.collective_compute(
                        "AllGather", ALU.bypass, replica_groups=PAIRS,
                        ins=[io["agwu_in"]], outs=[io["agwu_out"]]), "ccwu")
                    ctx.wait(g, "ccwg")
                    for c in range(DC):
                        ctx.inc(g.dma_start(
                            out=WGQ_T[c][:, 0:HI],
                            in_=io["agwg_out"][c * 128:(c + 1) * 128, :]),
                            "wfin", 16)
                        ctx.inc(g.dma_start(
                            out=WGQ_T[c][:, HI:I0],
                            in_=io["agwg_out"][D + c * 128:D + (c + 1) * 128, :]),
                            "wfin", 16)
                    ctx.wait(g, "ccwu")
                    for c in range(DC):
                        ctx.inc(g.dma_start(
                            out=WUQ_T[c][:, 0:HI],
                            in_=io["agwu_out"][c * 128:(c + 1) * 128, :]),
                            "wfin", 16)
                        ctx.inc(g.dma_start(
                            out=WUQ_T[c][:, HI:I0],
                            in_=io["agwu_out"][D + c * 128:D + (c + 1) * 128, :]),
                            "wfin", 16)

                @block.sync
                def _(sync):
                    ctx.wait(sync, "wush")
            with ExitStack() as wds:
                cd = sub_ctx(ctx, wds)
                TMPd = [cd.sb(f"WTMPd{i}", [128, 512], BF16)
                        for i in range(IC0)]
                emit_wquant(cd, block, io["wd_own"], 512, I0, TMPd, sc, "wd")

                @block.gpsimd
                def _(g):
                    ctx.wait(g, "wd_tp")
                    for j in range(IC0):
                        ctx.inc(g.dma_start(
                            out=io["agwd_in"][j * 128:(j + 1) * 128, :],
                            in_=TMPd[j][:, :]), "wdsh", 16)
                    ctx.wait(g, "wdsh")
                    ctx.inc(g.collective_compute(
                        "AllGather", ALU.bypass, replica_groups=PAIRS,
                        ins=[io["agwd_in"]], outs=[io["agwd_out"]]), "ccwd")
                    ctx.wait(g, "ccwd")
                    for j in range(IC0):
                        ctx.inc(g.dma_start(
                            out=WDQ_T[j][:, 0:512],
                            in_=io["agwd_out"][j * 128:(j + 1) * 128, :]),
                            "wfin", 16)
                        ctx.inc(g.dma_start(
                            out=WDQ_T[j][:, 512:D],
                            in_=io["agwd_out"][I0 + j * 128:I0 + (j + 1) * 128, :]),
                            "wfin", 16)
        else:
            emit_wquant(cf, block, io["w_gate"], I0, D, WGQ_T, sc, "wg")
            emit_wquant(cf, block, io["w_up"], I0, D, WUQ_T, sc, "wu")
            emit_wquant(cf, block, io["w_down"], D, I0, WDQ_T, sc, "wd")
        emit_barrier(ctx, block)

        with ExitStack() as p3s:
            c3 = sub_ctx(ctx, p3s)
            XQT = [c3.sb(f"XQT{i}", [128, T], BF16) for i in range(DC)]
            GSIL = c3.sb("GSIL", [128, NHALF], BF16)
            HT = [c3.sb(f"HT{i}", [128, I0], BF16) for i in range(2)]
            HQT = [c3.sb(f"HQT{i}", [128, I0], BF16) for i in range(1)]
            YF = [c3.sb(f"YF{i}", [128, D], BF16) for i in range(2)]
            PG = c3.ps("PG", [128, NHALF])
            PU = c3.ps("PU", [128, NHALF])
            PY = c3.ps("PY", [128, D])

            @block.sync
            def _(sync):
                for c in range(DC):
                    ctx.inc(sync.dma_start(
                        out=XQT[c][:, :],
                        in_=io["xqT_own"][c * 128:(c + 1) * 128, :]), "p3ld", 16)

            cch = [(c0, min(QW, I0 - c0)) for c0 in range(0, I0, QW)]
            vq_after, aq_after, tpq_after, dn_after, yst_after = {}, {}, {}, {}, {}
            for tt in range(TT):
                buf = tt % 2
                for nh in range(2):
                    @block.tensor
                    def _(tn, tt=tt, nh=nh):
                        if tt == 0 and nh == 0:
                            ctx.wait(tn, "p3ld")
                        ctx.wait(tn, "p3a")
                        ctx.wait(tn, "p3v")
                        for k in range(DC):
                            lhs = XQT[k][:, tt * 128:(tt + 1) * 128]
                            for (n0, nn) in NPARTS:
                                tn.matmul(out=PG[:, n0:n0 + nn], lhsT=lhs,
                                          rhs=WGQ_T[k][:, nh * NHALF + n0:
                                                       nh * NHALF + n0 + nn],
                                          start=(k == 0), stop=(k == DC - 1))
                            for (n0, nn) in NPARTS:
                                ins = tn.matmul(
                                    out=PU[:, n0:n0 + nn], lhsT=lhs,
                                    rhs=WUQ_T[k][:, nh * NHALF + n0:
                                                 nh * NHALF + n0 + nn],
                                    start=(k == 0), stop=(k == DC - 1))
                        ctx.inc(ins, "p3mm")

                    @block.scalar
                    def _(a, tt=tt, nh=nh):
                        ctx.wait(a, "p3mm")
                        ctx.wait(a, "p3v")
                        ctx.inc(a.activation(out=GSIL[:, :], in_=PG[:, :],
                                             func=AF.Silu), "p3a")

                    @block.vector
                    def _(v, tt=tt, nh=nh, buf=buf):
                        ctx.wait(v, "p3a")
                        if tt >= 2 and nh == 0:
                            ctx.wait(v, "p3tp", tpq_after[tt - 2])
                        ctx.inc(v.tensor_tensor(
                            out=HT[buf][:, nh * NHALF:(nh + 1) * NHALF],
                            in0=PU[:, :], in1=GSIL[:, :], op=ALU.mult), "p3v")

                for ci, (c0, w) in enumerate(cch):
                    gi = tt * len(cch) + ci

                    @block.scalar
                    def _(a, tt=tt, buf=buf, c0=c0, w=w, gi=gi):
                        ctx.wait(a, "p3v")
                        if gi >= 2:
                            ctx.wait(a, "p3q", vq_after.get(("c", gi - 2), 0))
                        ctx.inc(a.sign(
                            out=sc["SGN"][:, (gi % 2) * QW:(gi % 2) * QW + w],
                            in_=HT[buf][:, c0:c0 + w]), "p3sa")

                    @block.vector
                    def _(v, tt=tt, buf=buf, c0=c0, w=w, gi=gi):
                        ctx.wait(v, "p3sa")
                        ctx.inc(emit_fp4_dve(v, sc, HT[buf][:, c0:c0 + w],
                                             HT[buf][:, c0:c0 + w], w,
                                             sgn_buf=gi % 2), "p3q")
                    vq_after[("c", gi)] = ctx.cnt["p3q"]
                vq_after[tt] = ctx.cnt["p3q"]

                if io.get("dbg_park_h"):
                    @block.gpsimd
                    def _(g, tt=tt, buf=buf):
                        ctx.wait(g, "p3q")
                        ctx.inc(g.dma_start(
                            out=io["h_park"][tt * 128:(tt + 1) * 128, :],
                            in_=HT[buf][:, :]), "p3hdbg", 16)

                @block.sync
                def _(sync, tt=tt, buf=buf):
                    ctx.wait(sync, "p3q")
                    if tt >= 1:
                        ctx.wait(sync, "p3dn", dn_after[tt - 1])
                    for hc in range(IC0):
                        ctx.inc(sync.dma_start_transpose(
                            out=HQT[0][:, hc * 128:(hc + 1) * 128],
                            in_=HT[buf][:, hc * 128:(hc + 1) * 128]), "p3tp", 16)
                tpq_after[tt] = ctx.cnt["p3tp"]

                @block.tensor
                def _(tn, tt=tt, buf=buf):
                    ctx.wait(tn, "p3tp")
                    ctx.wait(tn, "p3yv")
                    for hc in range(IC0):
                        lhs = HQT[0][:, hc * 128:(hc + 1) * 128]
                        for n0 in (0, 512):
                            ins = tn.matmul(out=PY[:, n0:n0 + 512], lhsT=lhs,
                                            rhs=WDQ_T[hc][:, n0:n0 + 512],
                                            start=(hc == 0), stop=(hc == IC0 - 1))
                    ctx.inc(ins, "p3dn")
                dn_after[tt] = ctx.cnt["p3dn"]

                @block.vector
                def _(v, tt=tt, buf=buf):
                    ctx.wait(v, "p3dn")
                    if tt >= 2:
                        ctx.wait(v, "p3yst", yst_after[tt - 2])
                    ctx.inc(v.tensor_copy(out=YF[buf][:, :], in_=PY[:, :]),
                            "p3yv")

                @block.gpsimd
                def _(g, tt=tt, buf=buf):
                    ctx.wait(g, "p3yv")
                    ctx.inc(g.dma_start(
                        out=io["yffn_park"][tt * 128:(tt + 1) * 128, :],
                        in_=YF[buf][:, :]), "p3yst", 16)
                yst_after[tt] = ctx.cnt["p3yst"]

            emit_barrier(ctx, block)


def emit_l0_body(ctx, stack, block, io):
    nc = ctx.nc
    emit_l0_ffn(ctx, block, io)
    emit_l0_attn_section(ctx, block, io)
    emit_ln1_out(ctx, block, io)


def emit_l0_attn_section(ctx, block, io):
    nc = ctx.nc

    with ExitStack() as As:
        ca = sub_ctx(ctx, As)
        xnT = [ca.sb(f"xnT{c}", [128, S], BF16) for c in range(DC)]
        VT = [ca.sb(f"VT{i}", [128, NH * 65], BF16) for i in range(ST)]
        ATTNT = [ca.sb(f"ATTNT{c}", [128, T], BF16) for c in range(DC)]
        WATT = ca.sb("WATT", [128, NH * HD], BF16)
        EYE32 = ca.sb("EYE32", [128, 128])
        CBIAS = ca.sb("CBIAS0", [128, 1])

        with ExitStack() as p5s:
            c5 = sub_ctx(ctx, p5s)
            WVT = [c5.sb(f"WVT{i}", [128, D], BF16) for i in range(DC)]
            BIAS_T = c5.sb("BIAS_T", [128, D])
            WVBR = c5.sb("WVBR", [1, D])
            ONESF = c5.sb("ONESF", [1, 128])
            VROW = [c5.sb(f"VROW{i}", [128, D], BF16) for i in range(2)]
            PVX = c5.ps("PVX", [128, D])
            PBI = c5.ps("PBI", [128, D])

            @block.sync
            def _(sync):
                if "cc0" in ctx.cnt:
                    ctx.wait(sync, "cc0")
                for c in range(DC):
                    for (c0, c1, src) in io["xnT_loads"](c):
                        ctx.inc(sync.dma_start(out=xnT[c][:, c0:c1], in_=src),
                                "p5ld", 16)
                ctx.inc(sync.dma_start(out=WVBR[:, :], in_=io["wv_b"]), "p5ld", 16)
                ctx.inc(sync.dma_start(out=EYE32[:, :], in_=io["eye"]), "p5ld", 16)

            @block.gpsimd
            def _(g):
                for c in range(DC):
                    ctx.inc(g.dma_start(out=WVT[c][:, :],
                                        in_=io["wv_wT"][c * 128:(c + 1) * 128, :]),
                            "p5lg", 16)
                ctx.inc(g.dma_start(
                    out=WATT[0:64, :].rearrange("d (h e) -> d h e", h=NH),
                    in_=io["watt"].transpose([1, 0, 2])), "p5lg", 16)
                ctx.inc(g.dma_start(
                    out=WATT[64:128, :].rearrange("d (h e) -> d h e", h=NH),
                    in_=io["watt"].transpose([1, 0, 2])), "p5lg", 16)
                for st_ in range(ST):
                    ones_col = VT[st_][:, :].rearrange(
                        "p (h c) -> p h c", c=65)[:, :, 64:65]
                    ctx.inc(g.memset(ones_col, 1.0), "p5o")

            @block.vector
            def _(v):
                v.memset(CBIAS[:, :], -L0_SHIFT)
                ctx.inc(v.memset(ONESF[:, :], 1.0), "p5v")

            @block.tensor
            def _(tn):
                ctx.wait(tn, "p5ld")
                ctx.wait(tn, "p5lg")
                ctx.wait(tn, "p5v")
                for n0 in (0, 512):
                    ins = tn.matmul(out=PBI[:, n0:n0 + 512], lhsT=ONESF[:, :],
                                    rhs=WVBR[:, n0:n0 + 512], start=True,
                                    stop=True)
                ctx.inc(ins, "p5mm")

            @block.vector
            def _(v):
                ctx.wait(v, "p5mm")
                ctx.inc(v.tensor_copy(out=BIAS_T[:, :], in_=PBI[:, :]), "p5v")

            vrow_after = {}
            for st_ in range(ST):
                buf = st_ % 2

                @block.tensor
                def _(tn, st_=st_):
                    ctx.wait(tn, "p5v")
                    if st_ >= 1:
                        ctx.wait(tn, "p5ev", vrow_after[st_ - 1])
                    for c in range(DC):
                        lhs = xnT[c][:, st_ * 128:(st_ + 1) * 128]
                        for n0 in (0, 512):
                            ins = tn.matmul(out=PVX[:, n0:n0 + 512], lhsT=lhs,
                                            rhs=WVT[c][:, n0:n0 + 512],
                                            start=(c == 0), stop=(c == DC - 1))
                    ctx.inc(ins, "p5mm")

                @block.vector
                def _(v, st_=st_, buf=buf):
                    ctx.wait(v, "p5mm")
                    if st_ >= 2:
                        ctx.wait(v, "p5pc", NH * (st_ - 1))
                    ctx.inc(v.scalar_tensor_tensor(
                        out=VROW[buf][:, :], in0=PVX[:, :], scalar=1.0,
                        in1=BIAS_T[:, :], op0=ALU.mult, op1=ALU.add), "p5ev")

                @block.gpsimd
                def _(g, st_=st_, buf=buf):
                    ctx.wait(g, "p5ev")
                    for h in range(NH):
                        ctx.inc(g.tensor_copy(
                            out=VT[st_][:, 65 * h:65 * h + 64],
                            in_=VROW[buf][:, 64 * h:64 * h + 64]), "p5pc", 1)
                vrow_after[st_] = ctx.cnt["p5ev"]

            emit_barrier(ctx, block)

        emit_attn_pipelined(ctx, block, dict(
            mode="l0", tag="A0", ATTNT=ATTNT, EYE32=EYE32,
            keys_lhsT=lambda h, kc: xnT[h // 2][
                64 * (h % 2):64 * (h % 2) + 64, kc * 128:(kc + 1) * 128],
            v_lhsT=lambda kc, h: VT[kc][:, 65 * h:65 * h + 65],
            exp_bias=lambda h, kc: CBIAS[:, 0:1],
            qb_base=lambda h: 64 * (h % 2),
            qsrc_dram=lambda c, qc: io["xnT_own"][
                c * 128:(c + 1) * 128, qc * 1024:(qc + 1) * 1024],
            watt_lhsT=lambda h: WATT[64 * (h % 2):64 * (h % 2) + 64,
                                     64 * h:64 * h + 64],
        ))

        with ExitStack() as p8s:
            c8 = sub_ctx(ctx, p8s)
            OUTWT = [c8.sb(f"OUTWT{i}", [128, D], BF16) for i in range(DC)]
            OUTBR = c8.sb("OUTBR", [1, D])
            ONES8 = c8.sb("ONES8", [1, 128])
            XTL = [c8.sb(f"XTL{i}", [128, D]) for i in range(2)]
            YFB = [c8.sb(f"YFB{i}", [128, D], BF16) for i in range(2)]
            X1TL = [c8.sb(f"X1TL{i}", [128, D]) for i in range(2)]
            PYA = c8.ps("PYA", [128, D])

            @block.sync
            def _(sync):
                ctx.inc(sync.dma_start(out=OUTBR[:, :], in_=io["out_b"]),
                        "p8ld", 16)

            @block.gpsimd
            def _(g):
                for c in range(DC):
                    ctx.inc(g.dma_start(
                        out=OUTWT[c][:, :],
                        in_=io["out_wT"][c * 128:(c + 1) * 128, :]), "p8lg", 16)

            @block.vector
            def _(v):
                ctx.inc(v.memset(ONES8[:, :], 1.0), "p8v")

            x1_after, st_after = {}, {}
            for tt in range(TT):
                buf = tt % 2

                @block.sync
                def _(sync, tt=tt, buf=buf):
                    if tt >= 2:
                        ctx.wait(sync, "p8v", x1_after[tt - 2])
                    ctx.inc(sync.dma_start(
                        out=XTL[buf][:, :],
                        in_=io["x_in"][tt * 128:(tt + 1) * 128, :]), "p8xl", 16)
                    ctx.inc(sync.dma_start(
                        out=YFB[buf][:, :],
                        in_=io["yffn_park"][tt * 128:(tt + 1) * 128, :]),
                        "p8xl", 16)

                @block.tensor
                def _(tn, tt=tt):
                    ctx.wait(tn, "p8ld")
                    ctx.wait(tn, "p8lg")
                    ctx.wait(tn, "p8v")
                    for n0 in (0, 512):
                        tn.matmul(out=PYA[:, n0:n0 + 512], lhsT=ONES8[:, :],
                                  rhs=OUTBR[:, n0:n0 + 512], start=True,
                                  stop=False)
                    for c in range(DC):
                        lhs = ATTNT[c][:, tt * 128:(tt + 1) * 128]
                        for n0 in (0, 512):
                            ins = tn.matmul(out=PYA[:, n0:n0 + 512], lhsT=lhs,
                                            rhs=OUTWT[c][:, n0:n0 + 512],
                                            start=False, stop=(c == DC - 1))
                    ctx.inc(ins, "p8mm")

                @block.vector
                def _(v, tt=tt, buf=buf):
                    ctx.wait(v, "p8mm")
                    ctx.wait(v, "p8xl")
                    if tt >= 2:
                        ctx.wait(v, "p8st", st_after[tt - 2])
                    v.scalar_tensor_tensor(
                        out=X1TL[buf][:, :], in0=PYA[:, :], scalar=1.0,
                        in1=XTL[buf][:, :], op0=ALU.mult, op1=ALU.add)
                    ctx.inc(v.tensor_tensor(out=X1TL[buf][:, :],
                                            in0=X1TL[buf][:, :],
                                            in1=YFB[buf][:, :], op=ALU.add),
                            "p8v")
                x1_after[tt] = ctx.cnt["p8v"]

                @block.gpsimd
                def _(g, tt=tt, buf=buf):
                    ctx.wait(g, "p8v")
                    ctx.inc(g.dma_start(
                        out=io["x1_out"][tt * 128:(tt + 1) * 128, :],
                        in_=X1TL[buf][:, :]), "p8st", 16)
                st_after[tt] = ctx.cnt["p8st"]

            emit_barrier(ctx, block)


def emit_ln1_out(ctx, block, io):
    xn1T = [ctx.sb(f"xn1T{c}", [128, T], BF16) for c in range(DC)]
    tp = emit_ln_phase(ctx, block, io["x1_out"], xn1T, None, 0, "ln1")

    @block.gpsimd
    def _(g):
        ctx.wait(g, tp)
        for c in range(DC):
            ctx.inc(g.dma_start(out=io["xn1T_out"][c * 128:(c + 1) * 128, :],
                                in_=xn1T[c][:, :]), "stf", 16)
        ctx.wait(g, "stf")


def emit_attn_pipelined(ctx, block, cfg):
    nc = ctx.nc
    mode = cfg["mode"]
    tag = cfg["tag"]
    ATTNT = cfg["ATTNT"]
    EYE32 = cfg["EYE32"]
    NKC = ST

    sqk, spv, sa = f"{tag}qk", f"{tag}pv", f"{tag}a"
    sv_, stp, smq = f"{tag}v", f"{tag}tp", f"{tag}q"
    sql, src_ = f"{tag}ql", f"{tag}rc"

    for qc in range(2):
        with ExitStack() as qs:
            cq = sub_ctx(ctx, qs)
            Q0T = ([cq.sb(f"{tag}Q0T{qc}_{i}", [128, 1024], BF16)
                    for i in range(DC)] if mode == "l0" else None)
            for name in (sqk, spv, sa, sv_, stp, smq, sql, src_):
                ctx.sem(name)

            if mode == "l0":
                with ExitStack() as qps:
                    cqp = sub_ctx(ctx, qps)
                    QS_ALL = [cqp.sb(f"{tag}QS{qc}_{i}", [128, 1024], BF16)
                              for i in range(DC)]
                    PS_QP = [cqp.ps(f"{tag}QP{qc}_{i}", [64, 1024])
                             for i in range(2)]

                    @block.sync
                    def _(sync, qc=qc):
                        for c in range(DC):
                            ctx.inc(sync.dma_start(
                                out=QS_ALL[c][:, :],
                                in_=cfg["qsrc_dram"](c, qc)), sql, 16)
                    bQL0 = ctx.cnt[sql]
                    bQ0 = ctx.cnt[smq]
                    bV0 = ctx.cnt[sv_]

                    @block.tensor
                    def _(tn, qc=qc):
                        tn.wait_ge(ctx.sem(sql), bQL0)
                        for h in range(NH):
                            if h >= 2:
                                tn.wait_ge(ctx.sem(sv_), bV0 + h - 1)
                            lhsw = cfg["watt_lhsT"](h)
                            rhsq = QS_ALL[h // 2][64 * (h % 2):64 * (h % 2) + 64, :]
                            for n0 in (0, 512):
                                ins = tn.matmul(out=PS_QP[h % 2][:, n0:n0 + 512],
                                                lhsT=lhsw,
                                                rhs=rhsq[:, n0:n0 + 512],
                                                start=True, stop=True)
                            ctx.inc(ins, smq)

                    @block.vector
                    def _(v, qc=qc):
                        for h in range(NH):
                            v.wait_ge(ctx.sem(smq), bQ0 + h + 1)
                            ctx.inc(v.tensor_scalar(
                                out=Q0T[h // 2][64 * (h % 2):64 * (h % 2) + 64, :],
                                in0=PS_QP[h % 2][:, :], scalar1=0.125,
                                scalar2=None, op0=ALU.mult), sv_)
                    emit_barrier(ctx, block, engines=("tensor", "vector",
                                                      "scalar", "sync"))

            ATTN = [cq.sb(f"{tag}AT{qc}_{i}", [128, D], BF16) for i in range(8)]
            QB = (cq.sb(f"{tag}QB{qc}", [128, 1024], BF16)
                  if mode == "l1" else None)
            PT = [cq.sb(f"{tag}PT{qc}_{i}", [128, 1024], BF16) for i in range(2)]
            OT_SB = cq.sb(f"{tag}OT{qc}", [128, 1024])
            RECS = cq.sb(f"{tag}RC{qc}", [128, 1])
            PS_S = [cq.ps(f"{tag}SS{qc}_{i}", [128, 1024]) for i in range(2)]
            PS_O = cq.ps(f"{tag}SO{qc}", [128, 1024])
            PS_T = [cq.ps(f"{tag}ST{qc}_{i}", [128, 128]) for i in range(2)]

            bQK, bPV, bA = ctx.cnt[sqk], ctx.cnt[spv], ctx.cnt[sa]
            bV, bTP = ctx.cnt[sv_], ctx.cnt[stp]

            vec_events = []
            for h in range(NH):
                if mode == "l1":
                    vec_events += [("qcopy", h)]
                if h >= 1:
                    vec_events += [("evac", h - 1)]
                    vec_events += [("mult", h - 1, j) for j in range(8)]
            vec_events += [("evac", NH - 1)]
            vec_events += [("mult", NH - 1, j) for j in range(8)]
            VC = {ev: bV + i + 1 for i, ev in enumerate(vec_events)}

            def c_exp(g):
                return bA + g + 1

            def c_qk(g):
                return bQK + g + 1

            def c_pv(g):
                return bPV + g + 1

            def c_tp(h, j):
                return bTP + 8 * h + j + 1

            def q_rhs(h):
                if mode == "l0":
                    return Q0T[h // 2][64 * (h % 2):64 * (h % 2) + 64, :]
                qb0 = cfg["qb_base"](h)
                return QB[qb0:qb0 + 64, :]

            def tensor_head(tn, h):
                def emit_pv(g):
                    h_, kc_ = g // NKC, g % NKC
                    tn.wait_ge(ctx.sem(sa), c_exp(g))
                    if kc_ == 0 and h_ >= 1:
                        tn.wait_ge(ctx.sem(sv_), VC[("evac", h_ - 1)])
                    for n0 in (0, 512):
                        ins = tn.matmul(out=PS_O[0:65, n0:n0 + 512],
                                        lhsT=cfg["v_lhsT"](kc_, h_),
                                        rhs=PT[g % 2][:, n0:n0 + 512],
                                        start=(kc_ == 0), stop=(kc_ == NKC - 1))
                    ctx.inc(ins, spv)

                def emit_transposes(hh):
                    for j in range(8):
                        tn.wait_ge(ctx.sem(sv_), VC[("evac", hh)])
                        gj = 8 * hh + j - 2
                        if gj >= 0:
                            tn.wait_ge(ctx.sem(sv_),
                                       VC[("mult", gj // 8, gj % 8)])
                        ins = tn.transpose(out=PS_T[j % 2][:, 0:65],
                                           in_=OT_SB[0:65, j * 128:(j + 1) * 128],
                                           identity=EYE32[0:65, 0:65])
                        ctx.inc(ins, stp)

                rq = q_rhs(h)
                for kc in range(NKC):
                    g = NKC * h + kc
                    if kc == 0 and mode == "l1":
                        tn.wait_ge(ctx.sem(sv_), VC[("qcopy", h)])
                    if g >= 2:
                        tn.wait_ge(ctx.sem(sa), c_exp(g - 2))
                    lhs = cfg["keys_lhsT"](h, kc)
                    for n0 in (0, 512):
                        ins = tn.matmul(out=PS_S[g % 2][:, n0:n0 + 512],
                                        lhsT=lhs, rhs=rq[:, n0:n0 + 512],
                                        start=True, stop=True)
                    ctx.inc(ins, sqk)
                    if g >= 1:
                        emit_pv(g - 1)
                    if kc == 4 and h >= 1:
                        emit_transposes(h - 1)
                if h == NH - 1:
                    emit_pv(NKC * NH - 1)
                    emit_transposes(NH - 1)

            for h in range(NH):
                @block.tensor
                def _(tn, h=h):
                    tensor_head(tn, h)

            for g0 in range(0, NKC * NH, 128):
                @block.scalar
                def _(a, g0=g0):
                    for g in range(g0, g0 + 128):
                        h_, kc_ = g // NKC, g % NKC
                        a.wait_ge(ctx.sem(sqk), c_qk(g))
                        if g >= 2:
                            a.wait_ge(ctx.sem(spv), c_pv(g - 2))
                        ctx.inc(a.activation(out=PT[g % 2][:, :],
                                             in_=PS_S[g % 2][:, :], func=AF.Exp,
                                             bias=cfg["exp_bias"](h_, kc_),
                                             scale=1.0), sa)

            def vector_round(v, h, last=False):
                def emit_qprep(hh):
                    qb0 = cfg["qb_base"](hh)
                    if hh >= 1:
                        v.wait_ge(ctx.sem(sqk), bQK + NKC * hh)
                    ctx.inc(v.tensor_copy(out=QB[qb0:qb0 + 64, :],
                                          in_=cfg["q_sb"](hh, qc)), sv_)

                def emit_norm(hh):
                    v.wait_ge(ctx.sem(spv), c_pv(NKC * hh + NKC - 1))
                    if hh >= 1:
                        v.wait_ge(ctx.sem(stp), c_tp(hh - 1, 7))
                    ctx.inc(v.tensor_copy(out=OT_SB[0:65, :],
                                          in_=PS_O[0:65, :]), sv_)
                    for j in range(8):
                        v.wait_ge(ctx.sem(stp), c_tp(hh, j))
                        sl = PS_T[j % 2]
                        ctx.inc(v.reciprocal(out=RECS[:, 0:1],
                                             in_=sl[:, 64:65]), src_)
                        v.wait_ge(ctx.sem(src_), ctx.cnt[src_])
                        ctx.inc(v.tensor_scalar(
                            out=ATTN[j][:, 64 * h if False else 64 * hh:
                                        64 * hh + 64],
                            in0=sl[:, 0:64], scalar1=RECS[:, 0:1],
                            scalar2=None, op0=ALU.mult), sv_)

                if not last:
                    if mode == "l1":
                        emit_qprep(h)
                    if h >= 1:
                        emit_norm(h - 1)
                else:
                    emit_norm(NH - 1)

            for h in range(NH):
                @block.vector
                def _(v, h=h):
                    vector_round(v, h)

            @block.vector
            def _(v):
                vector_round(v, NH - 1, last=True)

            @block.sync
            def _(sync, qc=qc):
                for j in range(8):
                    t = qc * 8 + j
                    sync.wait_ge(ctx.sem(sv_), VC[("mult", NH - 1, j)])
                    for c in range(DC):
                        ctx.inc(sync.dma_start_transpose(
                            out=ATTNT[c][:, t * 128:(t + 1) * 128],
                            in_=ATTN[j][:, c * 128:(c + 1) * 128]), f"{tag}td", 16)

            emit_barrier(ctx, block)


def build_launch3():
    nc = bass.Bass("TRN2", target_bir_lowering=False)
    io = {}
    for name, shape, dt, kind in [
        ("x1_in", [T, D], F32, "ExternalInput"),
        ("xn1T_full", [D, S], BF16, "ExternalInput"),
        ("xn1T_own", [D, T], BF16, "ExternalInput"),
        ("Wq", [D, D], F32, "ExternalInput"),
        ("Wkv", [D, 512], F32, "ExternalInput"),
        ("Wo", [D, D], F32, "ExternalInput"),
        ("wgT", [I1, D], I32, "ExternalInput"),
        ("wuT", [I1, D], I32, "ExternalInput"),
        ("wdT", [I1, D], I32, "ExternalInput"),
        ("alibi", [128, NH * ST], F32, "ExternalInput"),
        ("eye", [128, 128], F32, "ExternalInput"),
        ("x2_out", [T, D], F32, "ExternalOutput"),
        ("yffn_park", [T, D], BF16, "Internal"),
    ]:
        io[name] = nc.dram_tensor(name, shape, dt, kind=kind).ap()
    io["xn1T_loads"] = lambda c: [(0, S, io["xn1T_full"][c * 128:(c + 1) * 128, :])]
    with ExitStack() as stack:
        ctx = Ctx(nc, stack)
        block = stack.enter_context(nc.Block())
        emit_l1_body(ctx, stack, block, io)
    return nc


def emit_l1_body(ctx, stack, block, io):
    nc = ctx.nc
    IC1 = I1 // 128

    with ExitStack() as As:
        ca = sub_ctx(ctx, As)
        K1T = [ca.sb(f"K1T{i}", [128, S], BF16) for i in range(2)]
        V1 = [ca.sb(f"V1_{i}", [128, NKV * 65], BF16) for i in range(ST)]
        ATTNT = [ca.sb(f"L1ATTNT{c}", [128, T], BF16) for c in range(DC)]
        EYE32 = ca.sb("L1EYE", [128, 128])
        ALIBI = ca.sb("ALIBI", [128, NH * ST])
        Q1T = [ca.sb(f"Q1T{i}", [128, T], BF16) for i in range(DC)]

        with ExitStack() as pk:
            ck = sub_ctx(ctx, pk)
            xn1T = [ck.sb(f"xn1T{c}", [128, S], BF16) for c in range(DC)]
            WKV = [ck.sb(f"WKV{c}", [128, 512], BF16) for c in range(DC)]
            KROW = [ck.sb(f"KROW{i}", [128, 256], BF16) for i in range(2)]
            VROW = [ck.sb(f"VROW1_{i}", [128, 256], BF16) for i in range(2)]
            PKV = ck.ps("PKV", [128, 512])

            @block.sync
            def _(sync):
                if "cc1" in ctx.cnt:
                    ctx.wait(sync, "cc1")
                for c in range(DC):
                    for (c0, c1, src) in io["xn1T_loads"](c):
                        ctx.inc(sync.dma_start(out=xn1T[c][:, c0:c1], in_=src),
                                "k1ld", 16)
                ctx.inc(sync.dma_start(out=EYE32[:, :], in_=io["eye"]),
                        "k1ld", 16)
                ctx.inc(sync.dma_start(out=ALIBI[:, :], in_=io["alibi"]),
                        "k1ld", 16)

            @block.gpsimd
            def _(g):
                for c in range(DC):
                    ctx.inc(g.dma_start(out=WKV[c][:, :],
                                        in_=io["Wkv"][c * 128:(c + 1) * 128, :]),
                            "k1lg", 16)
                for st_ in range(ST):
                    ones_col = V1[st_][:, :].rearrange(
                        "p (h c) -> p h c", c=65)[:, :, 64:65]
                    ctx.inc(g.memset(ones_col, 1.0), "k1o")

            krow_after = {}
            for st_ in range(ST):
                buf = st_ % 2

                @block.tensor
                def _(tn, st_=st_):
                    ctx.wait(tn, "k1ld")
                    ctx.wait(tn, "k1lg")
                    if st_ >= 2:
                        ctx.wait(tn, "k1tp", krow_after[st_ - 2][0])
                        ctx.wait(tn, "k1pc", krow_after[st_ - 2][1])
                    ctx.wait(tn, "k1v")
                    for c in range(DC):
                        ins = tn.matmul(out=PKV[:, :],
                                        lhsT=xn1T[c][:, st_ * 128:(st_ + 1) * 128],
                                        rhs=WKV[c][:, :],
                                        start=(c == 0), stop=(c == DC - 1))
                    ctx.inc(ins, "k1mm")

                @block.vector
                def _(v, st_=st_, buf=buf):
                    ctx.wait(v, "k1mm")
                    v.tensor_copy(out=KROW[buf][:, :], in_=PKV[:, 0:256])
                    ctx.inc(v.tensor_copy(out=VROW[buf][:, :],
                                          in_=PKV[:, 256:512]), "k1v")

                @block.sync
                def _(sync, st_=st_, buf=buf):
                    ctx.wait(sync, "k1v")
                    for half in range(2):
                        ctx.inc(sync.dma_start_transpose(
                            out=K1T[half][:, st_ * 128:(st_ + 1) * 128],
                            in_=KROW[buf][:, half * 128:(half + 1) * 128]),
                            "k1tp", 16)

                @block.gpsimd
                def _(g, st_=st_, buf=buf):
                    ctx.wait(g, "k1v")
                    for kv in range(NKV):
                        ctx.inc(g.tensor_copy(
                            out=V1[st_][:, 65 * kv:65 * kv + 64],
                            in_=VROW[buf][:, 64 * kv:64 * kv + 64]), "k1pc", 1)
                krow_after[st_] = (ctx.cnt["k1tp"], ctx.cnt["k1pc"])

            emit_barrier(ctx, block)

        with ExitStack() as pq:
            cqx = sub_ctx(ctx, pq)
            XO = [cqx.sb(f"XOQ{c}", [128, T], BF16) for c in range(DC)]
            WQSB = [cqx.sb(f"WQSB{c}", [128, D], BF16) for c in range(DC)]
            PQ1 = cqx.ps("PQ1", [128, 1024])

            @block.sync
            def _(sync):
                for c in range(DC):
                    ctx.inc(sync.dma_start(
                        out=XO[c][:, :],
                        in_=io["xn1T_own"][c * 128:(c + 1) * 128, :]), "q1ld", 16)

            @block.gpsimd
            def _(g):
                for c in range(DC):
                    ctx.inc(g.dma_start(out=WQSB[c][:, :],
                                        in_=io["Wq"][c * 128:(c + 1) * 128, :]),
                            "q1lg", 16)

            for oc in range(DC):
                for th in range(2):
                    @block.tensor
                    def _(tn, oc=oc, th=th):
                        ctx.wait(tn, "q1ld")
                        ctx.wait(tn, "q1v")
                        for ic in range(DC):
                            for n0 in (0, 512):
                                ins = tn.matmul(
                                    out=PQ1[:, n0:n0 + 512],
                                    lhsT=WQSB[ic][:, oc * 128:(oc + 1) * 128],
                                    rhs=XO[ic][:, th * 1024 + n0:
                                               th * 1024 + n0 + 512],
                                    start=(ic == 0), stop=(ic == DC - 1))
                        ctx.inc(ins, "q1mm")

                    @block.vector
                    def _(v, oc=oc, th=th):
                        ctx.wait(v, "q1mm")
                        ctx.inc(v.tensor_scalar(
                            out=Q1T[oc][:, th * 1024:(th + 1) * 1024],
                            in0=PQ1[:, :], scalar1=0.125, scalar2=None,
                            op0=ALU.mult), "q1v")
            emit_barrier(ctx, block)

        for th in range(2):
            with ExitStack() as ph:
                chh = sub_ctx(ctx, ph)
                HT1 = [chh.sb(f"HT1_{th}_{i}", [128, 1024], BF16)
                       for i in range(IC1)]

                with ExitStack() as pg:
                    cg = sub_ctx(ctx, pg)
                    XO2 = [cg.sb(f"XOF{th}_{c}", [128, 1024], BF16)
                           for c in range(DC)]
                    WGN = [cg.sb(f"WGN{th}_{i}", [128, D], BF16) for i in range(2)]
                    WUN = [cg.sb(f"WUN{th}_{i}", [128, D], BF16) for i in range(2)]
                    SIL1 = cg.sb(f"SIL1_{th}", [128, 1024], BF16)
                    PG1 = cg.ps(f"PG1_{th}", [128, 1024])
                    PU1 = cg.ps(f"PU1_{th}", [128, 1024])

                    @block.sync
                    def _(sync, th=th):
                        for c in range(DC):
                            ctx.inc(sync.dma_start(
                                out=XO2[c][:, :],
                                in_=io["xn1T_own"][c * 128:(c + 1) * 128,
                                                   th * 1024:(th + 1) * 1024]),
                                "f1ld", 16)

                    gu_after = {}
                    for nt in range(IC1):
                        buf = nt % 2

                        @block.gpsimd
                        def _(g, nt=nt, buf=buf, th=th):
                            if nt >= 2:
                                ctx.wait(g, "f1mm", gu_after[nt - 2])
                            ctx.inc(g.dma_start(
                                out=WGN[buf][:, :],
                                in_=io["wgT"][nt * 128:(nt + 1) * 128, :]),
                                "f1wl", 16)
                            ctx.inc(g.dma_start(
                                out=WUN[buf][:, :],
                                in_=io["wuT"][nt * 128:(nt + 1) * 128, :]),
                                "f1wl", 16)

                        @block.tensor
                        def _(tn, nt=nt, buf=buf):
                            ctx.wait(tn, "f1ld")
                            ctx.wait(tn, "f1wl")
                            ctx.wait(tn, "f1a")
                            ctx.wait(tn, "f1v")
                            for ic in range(DC):
                                for n0 in (0, 512):
                                    tn.matmul(
                                        out=PG1[:, n0:n0 + 512],
                                        lhsT=WGN[buf][:, ic * 128:(ic + 1) * 128],
                                        rhs=XO2[ic][:, n0:n0 + 512],
                                        start=(ic == 0), stop=(ic == DC - 1))
                            for ic in range(DC):
                                for n0 in (0, 512):
                                    ins = tn.matmul(
                                        out=PU1[:, n0:n0 + 512],
                                        lhsT=WUN[buf][:, ic * 128:(ic + 1) * 128],
                                        rhs=XO2[ic][:, n0:n0 + 512],
                                        start=(ic == 0), stop=(ic == DC - 1))
                            ctx.inc(ins, "f1mm")
                        gu_after[nt] = ctx.cnt["f1mm"]

                        @block.scalar
                        def _(a, nt=nt):
                            ctx.wait(a, "f1mm")
                            ctx.wait(a, "f1v")
                            ctx.inc(a.activation(out=SIL1[:, :], in_=PG1[:, :],
                                                 func=AF.Silu,
                                                 scale=1.0 / 127.0), "f1a")

                        @block.vector
                        def _(v, nt=nt):
                            ctx.wait(v, "f1a")
                            ctx.inc(v.scalar_tensor_tensor(
                                out=HT1[nt][:, :], in0=PU1[:, :],
                                scalar=1.0 / 16129.0, in1=SIL1[:, :],
                                op0=ALU.mult, op1=ALU.mult), "f1v")
                    emit_barrier(ctx, block)

                with ExitStack() as pd:
                    cd = sub_ctx(ctx, pd)
                    WDN = [cd.sb(f"WDN{th}_{i}", [128, D], BF16)
                           for i in range(IC1)]
                    YF1 = [cd.sb(f"YF1_{th}_{i}", [128, D], BF16)
                           for i in range(2)]
                    PY1 = cd.ps(f"PY1_{th}", [128, D])

                    @block.gpsimd
                    def _(g, th=th):
                        for hc in range(IC1):
                            ctx.inc(g.dma_start(
                                out=WDN[hc][:, :],
                                in_=io["wdT"][hc * 128:(hc + 1) * 128, :]),
                                "d1wl", 16)

                    yst_after = {}
                    for ti in range(8):
                        tt = th * 8 + ti
                        buf = ti % 2

                        @block.tensor
                        def _(tn, ti=ti):
                            ctx.wait(tn, "d1wl")
                            ctx.wait(tn, "d1v")
                            for hc in range(IC1):
                                for n0 in (0, 512):
                                    ins = tn.matmul(
                                        out=PY1[:, n0:n0 + 512],
                                        lhsT=HT1[hc][:, ti * 128:(ti + 1) * 128],
                                        rhs=WDN[hc][:, n0:n0 + 512],
                                        start=(hc == 0), stop=(hc == IC1 - 1))
                            ctx.inc(ins, "d1mm")

                        @block.vector
                        def _(v, ti=ti, buf=buf):
                            ctx.wait(v, "d1mm")
                            if ti >= 2:
                                ctx.wait(v, "d1yst", yst_after[ti - 2])
                            ctx.inc(v.tensor_copy(out=YF1[buf][:, :],
                                                  in_=PY1[:, :]), "d1v")

                        @block.gpsimd
                        def _(g, tt=tt, buf=buf):
                            ctx.wait(g, "d1v")
                            ctx.inc(g.dma_start(
                                out=io["yffn_park"][tt * 128:(tt + 1) * 128, :],
                                in_=YF1[buf][:, :]), "d1yst", 16)
                        yst_after[ti] = ctx.cnt["d1yst"]
                    emit_barrier(ctx, block)

        emit_attn_pipelined(ctx, block, dict(
            mode="l1", tag="A1", ATTNT=ATTNT, EYE32=EYE32,
            keys_lhsT=lambda h, kc: K1T[(h // 4) // 2][
                64 * ((h // 4) % 2):64 * ((h // 4) % 2) + 64,
                kc * 128:(kc + 1) * 128],
            v_lhsT=lambda kc, h: V1[kc][:, 65 * (h // 4):65 * (h // 4) + 65],
            exp_bias=lambda h, kc: ALIBI[:, h * ST + kc:h * ST + kc + 1],
            qb_base=lambda h: 64 * ((h // 4) % 2),
            q_sb=lambda h, qc: Q1T[h // 2][
                64 * (h % 2):64 * (h % 2) + 64, qc * 1024:(qc + 1) * 1024],
        ))

        with ExitStack() as p8s:
            c8 = sub_ctx(ctx, p8s)
            WOSB = [c8.sb(f"WOSB{i}", [128, D], BF16) for i in range(DC)]
            XTL = [c8.sb(f"L1XTL{i}", [128, D]) for i in range(2)]
            YFB = [c8.sb(f"L1YFB{i}", [128, D], BF16) for i in range(2)]
            X2TL = [c8.sb(f"X2TL{i}", [128, D]) for i in range(2)]
            PYA = c8.ps("L1PYA", [128, D])

            @block.gpsimd
            def _(g):
                for c in range(DC):
                    ctx.inc(g.dma_start(out=WOSB[c][:, :],
                                        in_=io["Wo"][c * 128:(c + 1) * 128, :]),
                            "o1ld", 16)

            x2_after, st_after = {}, {}
            for tt in range(TT):
                buf = tt % 2

                @block.sync
                def _(sync, tt=tt, buf=buf):
                    if tt >= 2:
                        ctx.wait(sync, "o1v", x2_after[tt - 2])
                    ctx.inc(sync.dma_start(
                        out=XTL[buf][:, :],
                        in_=io["x1_in"][tt * 128:(tt + 1) * 128, :]), "o1xl", 16)
                    ctx.inc(sync.dma_start(
                        out=YFB[buf][:, :],
                        in_=io["yffn_park"][tt * 128:(tt + 1) * 128, :]),
                        "o1xl", 16)

                @block.tensor
                def _(tn, tt=tt):
                    ctx.wait(tn, "o1ld")
                    ctx.wait(tn, "o1v")
                    for c in range(DC):
                        lhs = ATTNT[c][:, tt * 128:(tt + 1) * 128]
                        for n0 in (0, 512):
                            ins = tn.matmul(out=PYA[:, n0:n0 + 512], lhsT=lhs,
                                            rhs=WOSB[c][:, n0:n0 + 512],
                                            start=(c == 0), stop=(c == DC - 1))
                    ctx.inc(ins, "o1mm")

                @block.vector
                def _(v, tt=tt, buf=buf):
                    ctx.wait(v, "o1mm")
                    ctx.wait(v, "o1xl")
                    if tt >= 2:
                        ctx.wait(v, "o1st", st_after[tt - 2])
                    v.scalar_tensor_tensor(
                        out=X2TL[buf][:, :], in0=PYA[:, :], scalar=1.0,
                        in1=XTL[buf][:, :], op0=ALU.mult, op1=ALU.add)
                    ctx.inc(v.tensor_tensor(out=X2TL[buf][:, :],
                                            in0=X2TL[buf][:, :],
                                            in1=YFB[buf][:, :], op=ALU.add),
                            "o1v")
                x2_after[tt] = ctx.cnt["o1v"]

                @block.gpsimd
                def _(g, tt=tt, buf=buf):
                    ctx.wait(g, "o1v")
                    ctx.inc(g.dma_start(
                        out=io["x2_out"][tt * 128:(tt + 1) * 128, :],
                        in_=X2TL[buf][:, :]), "o1st", 16)
                st_after[tt] = ctx.cnt["o1st"]

            @block.gpsimd
            def _(g):
                ctx.wait(g, "o1st")


def build_fused(debug=False):
    nc = bass.Bass("TRN2", target_bir_lowering=False)
    io = {}
    for name, shape, dt, kind in [
        ("x_in", [T, D], F32, "ExternalInput"),
        ("wg_own", [I0 // 2, D], F32, "ExternalInput"),
        ("wu_own", [I0 // 2, D], F32, "ExternalInput"),
        ("wd_own", [512, I0], F32, "ExternalInput"),
        ("agwd_in", [I0, 512], BF16, "Internal"),
        ("agwd_out", [2 * I0, 512], BF16, "Internal"),
        ("agwg_in", [D, I0 // 2], BF16, "Internal"),
        ("agwg_out", [2 * D, I0 // 2], BF16, "Internal"),
        ("agwu_in", [D, I0 // 2], BF16, "Internal"),
        ("agwu_out", [2 * D, I0 // 2], BF16, "Internal"),
        ("wv_wT", [D, D], F32, "ExternalInput"),
        ("wv_b", [1, D], F32, "ExternalInput"),
        ("out_wT", [D, D], F32, "ExternalInput"),
        ("out_b", [1, D], F32, "ExternalInput"),
        ("watt", [NH, HD, HD], F32, "ExternalInput"),
        ("eye", [128, 128], F32, "ExternalInput"),
        ("Wq", [D, D], F32, "ExternalInput"),
        ("Wkv", [D, 512], F32, "ExternalInput"),
        ("Wo", [D, D], F32, "ExternalInput"),
        ("wgT", [I1, D], I32, "ExternalInput"),
        ("wuT", [I1, D], I32, "ExternalInput"),
        ("wdT", [I1, D], I32, "ExternalInput"),
        ("alibi", [128, NH * ST], F32, "ExternalInput"),
        ("x2_out", [T, D], F32, "ExternalOutput"),
        ("h_park", [T, I0], BF16, "Internal"),
        ("yffn_park", [T, D], BF16, "Internal"),
        ("yffn1_park", [T, D], BF16, "Internal"),
        ("x1_park", [T, D], F32, "Internal"),
        ("xq_park", [D, T], BF16, "Internal"),
        ("ag0_in", [D, T], BF16, "Internal"),
        ("ag0_out", [2 * D, T], BF16, "Internal"),
        ("ag1_in", [D, T], BF16, "Internal"),
        ("ag1_out", [2 * D, T], BF16, "Internal"),
    ] + ([("dbg_x1", [T, D], F32, "ExternalOutput"),
          ("dbg_yffn", [T, D], BF16, "ExternalOutput"),
          ("dbg_xq", [D, T], BF16, "ExternalOutput"),
          ("dbg_h", [T, I0], BF16, "ExternalOutput")] if debug else []):
        io[name] = nc.dram_tensor(name, shape, dt, kind=kind).ap()

    PAIRS = [[0, 1], [2, 3], [4, 5], [6, 7]]

    with ExitStack() as stack:
        ctx = Ctx(nc, stack)
        block = stack.enter_context(nc.Block())

        with ExitStack() as s0:
            c0 = sub_ctx(ctx, s0)
            xnT0 = [c0.sb(f"f_xnT0_{c}", [128, T], BF16) for c in range(DC)]
            xqT0 = [c0.sb(f"f_xqT0_{c}", [128, T], BF16) for c in range(DC)]
            tp = emit_ln_phase(c0, block, io["x_in"], xnT0, xqT0, 0, "ln0")

            @block.gpsimd
            def _(g):
                ctx.wait(g, tp)
                ctx.wait(g, "ln0_tpq")
                for c in range(DC):
                    ctx.inc(g.dma_start(
                        out=io["ag0_in"][c * 128:(c + 1) * 128, :],
                        in_=xnT0[c][:, :]), "ag0st", 16)
                    ctx.inc(g.dma_start(
                        out=io["xq_park"][c * 128:(c + 1) * 128, :],
                        in_=xqT0[c][:, :]), "ag0st", 16)
                ctx.wait(g, "ag0st")
                ctx.inc(g.collective_compute(
                    "AllGather", ALU.bypass, replica_groups=PAIRS,
                    ins=[io["ag0_in"]], outs=[io["ag0_out"]]), "cc0")
            emit_barrier(ctx, block, engines=("sync", "vector", "scalar"),
                         exclude=("cc0",))

        io_l0 = dict(io)
        io_l0["xqT_own"] = io["xq_park"]
        io_l0["xnT_own"] = io["ag0_in"]
        io_l0["x1_out"] = io["x1_park"]
        io_l0["xnT_loads"] = lambda c: [
            (0, T, io["ag0_out"][c * 128:(c + 1) * 128, :]),
            (T, S, io["ag0_out"][D + c * 128:D + (c + 1) * 128, :]),
        ]
        emit_l0_ffn(ctx, block, io_l0)
        emit_l0_attn_section(ctx, block, io_l0)

        with ExitStack() as s1:
            c1 = sub_ctx(ctx, s1)
            xn1T0 = [c1.sb(f"f_xn1T_{c}", [128, T], BF16) for c in range(DC)]
            tp1 = emit_ln_phase(c1, block, io["x1_park"], xn1T0, None, 0, "ln1")

            @block.gpsimd
            def _(g):
                ctx.wait(g, tp1)
                for c in range(DC):
                    ctx.inc(g.dma_start(
                        out=io["ag1_in"][c * 128:(c + 1) * 128, :],
                        in_=xn1T0[c][:, :]), "ag1st", 16)
                ctx.wait(g, "ag1st")
                ctx.inc(g.collective_compute(
                    "AllGather", ALU.bypass, replica_groups=PAIRS,
                    ins=[io["ag1_in"]], outs=[io["ag1_out"]]), "cc1")
            emit_barrier(ctx, block, engines=("sync", "vector", "scalar"),
                         exclude=("cc1",))

        io_l1 = dict(io)
        io_l1["x1_in"] = io["x1_park"]
        io_l1["xn1T_own"] = io["ag1_in"]
        io_l1["yffn_park"] = io["yffn1_park"]
        io_l1["xn1T_loads"] = lambda c: [
            (0, T, io["ag1_out"][c * 128:(c + 1) * 128, :]),
            (T, S, io["ag1_out"][D + c * 128:D + (c + 1) * 128, :]),
        ]
        emit_l1_body(ctx, stack, block, io_l1)
        if debug:
            @block.gpsimd
            def _(g):
                ctx.wait(g, "o1st")
                ctx.inc(g.dma_start(out=io["dbg_x1"], in_=io["x1_park"]), "dbgs", 16)
                ctx.inc(g.dma_start(out=io["dbg_yffn"], in_=io["yffn_park"]), "dbgs", 16)
                ctx.inc(g.dma_start(out=io["dbg_xq"], in_=io["xq_park"]), "dbgs", 16)
                ctx.inc(g.dma_start(out=io["dbg_h"], in_=io["h_park"]), "dbgs", 16)
                ctx.wait(g, "dbgs")
    return nc


def kernel_fused(L0_attn_x, L0_attn_w_att_val, L0_attn_wv_w, L0_attn_wv_b,
                 L0_attn_out_w, L0_attn_out_b, L0_ffn_w_gate, L0_ffn_w_up,
                 L0_ffn_w_down, L1_attn_Wq, L1_attn_Wkv, L1_attn_Wo,
                 L1_ffn_w_gate, L1_ffn_w_up, L1_ffn_w_down):
    x = np.ascontiguousarray(np.asarray(L0_attn_x, dtype=np.float32))
    cores = list(range(8))
    wv_wT = np.ascontiguousarray(np.asarray(L0_attn_wv_w, dtype=np.float32).T)
    out_wT = np.ascontiguousarray(np.asarray(L0_attn_out_w, dtype=np.float32).T)
    wv_b = np.ascontiguousarray(np.asarray(L0_attn_wv_b,
                                           dtype=np.float32).reshape(1, D))
    out_b = np.ascontiguousarray(np.asarray(L0_attn_out_b,
                                            dtype=np.float32).reshape(1, D))
    watt = np.ascontiguousarray(np.asarray(L0_attn_w_att_val, dtype=np.float32))
    w_gate = np.ascontiguousarray(np.asarray(L0_ffn_w_gate, dtype=np.float32))
    w_up = np.ascontiguousarray(np.asarray(L0_ffn_w_up, dtype=np.float32))
    w_down = np.ascontiguousarray(np.asarray(L0_ffn_w_down, dtype=np.float32))
    Wq = np.ascontiguousarray(np.asarray(L1_attn_Wq, dtype=np.float32))
    Wkv = np.ascontiguousarray(np.asarray(L1_attn_Wkv, dtype=np.float32))
    Wo = np.ascontiguousarray(np.asarray(L1_attn_Wo, dtype=np.float32))
    def _lhsT_shuffle(w):
        a = np.asarray(w, dtype=np.int32).reshape(I1 // 128, 128, DC, 128)
        return np.ascontiguousarray(a.transpose(0, 3, 2, 1).reshape(I1, D))
    wgT = _lhsT_shuffle(L1_ffn_w_gate)
    wuT = _lhsT_shuffle(L1_ffn_w_up)
    wdT = np.ascontiguousarray(np.asarray(L1_ffn_w_down, dtype=np.int32).T)
    alibi = make_alibi_table()
    eye = np.eye(128, dtype=np.float32)

    nc = build_fused()
    maps = []
    for c in cores:
        bi, hf = c // 2, c % 2
        half = I0 // 2
        wg_own = w_gate[:half] if c % 2 == 0 else w_gate[half:]
        wu_own = w_up[:half] if c % 2 == 0 else w_up[half:]
        wd_own = w_down[:512] if c % 2 == 0 else w_down[512:]
        maps.append({
            "x_in": np.ascontiguousarray(x[bi, hf * T:(hf + 1) * T, :]),
            "wg_own": np.ascontiguousarray(wg_own),
            "wu_own": np.ascontiguousarray(wu_own),
            "wd_own": np.ascontiguousarray(wd_own),
            "wv_wT": wv_wT, "wv_b": wv_b, "out_wT": out_wT, "out_b": out_b,
            "watt": watt, "eye": eye,
            "Wq": Wq, "Wkv": Wkv, "Wo": Wo,
            "wgT": wgT, "wuT": wuT, "wdT": wdT, "alibi": alibi,
        })
    res = run_bass_kernel_spmd(nc, maps, core_ids=cores).results
    out = np.empty((4, S, D), dtype=np.float32)
    for c in cores:
        bi, hf = c // 2, c % 2
        out[bi, hf * T:(hf + 1) * T, :] = np.asarray(res[c]["x2_out"])
    return out


kernel = kernel_fused


def build_attn_test(**flags):
    nc = bass.Bass("TRN2", target_bir_lowering=False)
    io = {}
    for name, shape, dt, kind in [
        ("k1t_in", [2 * 128, S], BF16, "ExternalInput"),
        ("v1_in", [ST * 128, NKV * 65], BF16, "ExternalInput"),
        ("q1t_in", [D, T], BF16, "ExternalInput"),
        ("alibi", [128, NH * ST], F32, "ExternalInput"),
        ("eye", [128, 128], F32, "ExternalInput"),
        ("attnt_out", [D, T], BF16, "ExternalOutput"),
    ]:
        io[name] = nc.dram_tensor(name, shape, dt, kind=kind).ap()
    with ExitStack() as stack:
        ctx = Ctx(nc, stack)
        block = stack.enter_context(nc.Block())
        K1T = [ctx.sb(f"K1T{i}", [128, S], BF16) for i in range(2)]
        V1 = [ctx.sb(f"V1_{i}", [128, NKV * 65], BF16) for i in range(ST)]
        Q1T = [ctx.sb(f"Q1T{i}", [128, T], BF16) for i in range(DC)]
        ATTNT = [ctx.sb(f"ATTNT{c}", [128, T], BF16) for c in range(DC)]
        EYE32 = ctx.sb("EYE32", [128, 128])
        ALIBI = ctx.sb("ALIBI", [128, NH * ST])

        @block.sync
        def _(sync):
            for i in range(2):
                ctx.inc(sync.dma_start(out=K1T[i][:, :],
                                       in_=io["k1t_in"][i * 128:(i + 1) * 128, :]),
                        "ld", 16)
            for i in range(ST):
                ctx.inc(sync.dma_start(out=V1[i][:, :],
                                       in_=io["v1_in"][i * 128:(i + 1) * 128, :]),
                        "ld", 16)
            for c in range(DC):
                ctx.inc(sync.dma_start(out=Q1T[c][:, :],
                                       in_=io["q1t_in"][c * 128:(c + 1) * 128, :]),
                        "ld", 16)
            ctx.inc(sync.dma_start(out=EYE32[:, :], in_=io["eye"]), "ld", 16)
            ctx.inc(sync.dma_start(out=ALIBI[:, :], in_=io["alibi"]), "ld", 16)
        emit_barrier(ctx, block)

        emit_attn_pipelined(ctx, block, dict(
            mode="l1", tag="A1", ATTNT=ATTNT, EYE32=EYE32,
            keys_lhsT=lambda h, kc: K1T[(h // 4) // 2][
                64 * ((h // 4) % 2):64 * ((h // 4) % 2) + 64,
                kc * 128:(kc + 1) * 128],
            v_lhsT=lambda kc, h: V1[kc][:, 65 * (h // 4):65 * (h // 4) + 65],
            exp_bias=lambda h, kc: ALIBI[:, h * ST + kc:h * ST + kc + 1],
            qb_base=lambda h: 64 * ((h // 4) % 2),
            q_sb=lambda h, qc: Q1T[h // 2][
                64 * (h % 2):64 * (h % 2) + 64, qc * 1024:(qc + 1) * 1024],
            **flags,
        ))

        @block.gpsimd
        def _(g):
            for k, v in ctx.cnt.items():
                if v > 0:
                    g.wait_ge(ctx.sem(k), v)
            for c in range(DC):
                ctx.inc(g.dma_start(out=io["attnt_out"][c * 128:(c + 1) * 128, :],
                                    in_=ATTNT[c][:, :]), "stf", 16)
            ctx.wait(g, "stf")
    return nc


# version 12
# speedup vs baseline: 1.0394x; 1.0394x over previous
import numpy as np
import ml_dtypes
from contextlib import ExitStack

import concourse.bass as bass
import concourse.mybir as mybir
from concourse.bass_utils import run_bass_kernel_spmd

F32 = mybir.dt.float32
BF16 = mybir.dt.bfloat16
I32 = mybir.dt.int32
AF = mybir.ActivationFunctionType
ALU = mybir.AluOpType
AX = mybir.AxisListType

T = 2048
S = 4096
D = 1024
NH = 16
HD = 64
NKV = 4
I0 = 2816
I1 = 3072
TT = T // 128
ST = S // 128
DC = D // 128
L0_SHIFT = 32.0
BF = ml_dtypes.bfloat16


def alibi_slopes():
    x = (2.0 ** 8) ** (1.0 / NH)
    return np.asarray([1.0 / x ** (i + 1) for i in range(NH)], dtype=np.float64)


def make_alibi_table():
    sl = alibi_slopes()
    tbl = np.zeros((128, NH * ST), dtype=np.float64)
    p = np.arange(128)
    for h in range(NH):
        M = sl[h] * (S - 1) + 8.0
        for c in range(ST):
            tbl[:, h * ST + c] = sl[h] * (128 * c + p) - M
    return np.ascontiguousarray(tbl.astype(np.float32))


class Ctx:

    def __init__(self, nc, stack):
        self.nc = nc
        self.stack = stack
        self.sems = {}
        self.cnt = {}

    def sem(self, name):
        if name not in self.sems:
            self.sems[name] = self.stack.enter_context(self.nc.semaphore(name))
            self.cnt[name] = 0
        return self.sems[name]

    def inc(self, instr, name, k=1):
        instr.then_inc(self.sem(name), k)
        self.cnt[name] += k

    def wait(self, eng, name, val=None):
        eng.wait_ge(self.sem(name), self.cnt[name] if val is None else val)

    def sb(self, name, shape, dt=F32):
        return self.stack.enter_context(self.nc.sbuf_tensor(name, shape, dt))

    def ps(self, name, shape, dt=F32):
        return self.stack.enter_context(self.nc.psum_tensor(name, shape, dt))


def mk_quant_scratch(ctx, tag, n):
    return {"SC1": ctx.sb(f"{tag}_SC1", [128, n]),
            "SC2": ctx.sb(f"{tag}_SC2", [128, n]),
            "SCI": ctx.sb(f"{tag}_SCI", [128, n], I32),
            "SGN": ctx.sb(f"{tag}_SGN", [128, 2 * n], BF16),
            "BM": ctx.sb(f"{tag}_BM", [128, n // 16]),
            "SCL": ctx.sb(f"{tag}_SCL", [128, n // 16]),
            "INV": ctx.sb(f"{tag}_INV", [128, n // 16])}


def emit_fp4_dve(v, sc, src, dst, n, sgn_buf=0):
    nb = n // 16
    src3 = src.rearrange("p (b w) -> p b w", w=16)
    BM, SCL, INV = sc["BM"][:, 0:nb], sc["SCL"][:, 0:nb], sc["INV"][:, 0:nb]
    SC1, SC2 = sc["SC1"][:, 0:n], sc["SC2"][:, 0:n]
    nsg = sc["SGN"].ap().shape[1] // 2
    SCI = sc["SCI"][:, 0:n]
    SGN = sc["SGN"][:, sgn_buf * nsg:sgn_buf * nsg + n]
    sc1_3 = SC1.rearrange("p (b w) -> p b w", w=16)
    sc2_3 = SC2.rearrange("p (b w) -> p b w", w=16)
    v.drain()
    v.tensor_reduce(out=BM, in_=src3, axis=AX.X, op=ALU.max,
                    apply_absolute_value=True)
    v.drain()
    v.tensor_scalar(out=SCL, in0=BM, scalar1=1.0 / 6.0, scalar2=1e-12,
                    op0=ALU.mult, op1=ALU.max)
    v.drain()
    v.reciprocal(out=INV, in_=SCL)
    v.drain()
    invb = INV.unsqueeze(2).broadcast_to([128, nb, 16])
    sclb = SCL.unsqueeze(2).broadcast_to([128, nb, 16])
    v.tensor_tensor(out=sc1_3, in0=src3, in1=invb, op=ALU.mult)
    v.scalar_tensor_tensor(out=SC2, in0=SC1, scalar=-1.0, in1=SC1,
                           op0=ALU.mult, op1=ALU.max)
    v.tensor_copy(out=SCI, in_=SC2)
    v.tensor_scalar(out=SC1, in0=SCI, scalar1=-2.0, scalar2=0.0,
                    op0=ALU.add, op1=ALU.max)
    v.tensor_scalar(out=SC1, in0=SC1, scalar1=2.0, scalar2=None,
                    op0=ALU.min)
    v.tensor_scalar(out=SCI, in0=SC2, scalar1=2.0, scalar2=None,
                    op0=ALU.mult)
    v.tensor_scalar(out=SCI, in0=SCI, scalar1=4, scalar2=None, op0=ALU.min)
    v.scalar_tensor_tensor(out=SC1, in0=SCI, scalar=0.5, in1=SC1,
                           op0=ALU.mult, op1=ALU.add)
    v.tensor_scalar(out=SCI, in0=SC2, scalar1=0.5, scalar2=None,
                    op0=ALU.mult)
    v.tensor_scalar(out=SC2, in0=SCI, scalar1=-2.0, scalar2=0.0,
                    op0=ALU.add, op1=ALU.max)
    v.scalar_tensor_tensor(out=SC2, in0=SC2, scalar=2.0, in1=SC1,
                           op0=ALU.mult, op1=ALU.add)
    v.tensor_tensor(out=sc2_3, in0=sc2_3, in1=sclb, op=ALU.mult)
    return v.tensor_tensor(out=dst, in0=SC2, in1=SGN, op=ALU.mult)


def np_fp4_quant(x):
    x = np.asarray(x, dtype=np.float64)
    shp = x.shape
    xb = x.reshape(-1, 16)
    scl = np.maximum(np.abs(xb).max(axis=1, keepdims=True) / 6.0, 1e-12)
    inv = (1.0 / scl.astype(np.float32)).astype(np.float64)
    a = np.abs(xb * inv)
    q = (np.minimum(np.rint(2 * a), 4) * 0.5
         + np.minimum(np.maximum(np.rint(a) - 2, 0), 2)
         + 2 * np.maximum(np.rint(a / 2) - 2, 0))
    return (q * scl * np.sign(xb)).reshape(shp)


def emit_ln_phase(ctx, block, x_src, xnT, xqT, own_off, tag):
    quant = xqT is not None
    XT = [ctx.sb(f"{tag}_XT{i}", [128, D]) for i in range(TT)]
    STATS = ctx.sb(f"{tag}_STATS", [128, 4 * TT])
    XN = [ctx.sb(f"{tag}_XN{i}", [128, D]) for i in range(2)]
    XNB = [ctx.sb(f"{tag}_XNB{i}", [128, D], BF16) for i in range(2)]
    SQT = ctx.sb(f"{tag}_SQT", [128, D])
    sc = mk_quant_scratch(ctx, tag + "_q", D) if quant else None
    XQB = [ctx.sb(f"{tag}_XQB{i}", [128, D], BF16) for i in range(2)] if quant else None

    ld, vsem, asem, tp = (f"{tag}_ld", f"{tag}_v", f"{tag}_a", f"{tag}_tp")
    tpq = f"{tag}_tpq"
    ctx.sem(tp)
    ctx.sem(tpq)
    tp_after = {}
    asem_after = {}

    @block.sync
    def _(sync):
        for tt in range(TT):
            ctx.inc(sync.dma_start(out=XT[tt][:, :],
                                   in_=x_src[tt * 128:(tt + 1) * 128, :]), ld, 16)

    @block.vector
    def _(v):
        ctx.wait(v, ld)
        for tt in range(TT):
            v.tensor_reduce(out=STATS[:, tt:tt + 1], in_=XT[tt][:, :],
                            axis=AX.X, op=ALU.add)
            v.tensor_tensor(out=SQT[:, :], in0=XT[tt][:, :], in1=XT[tt][:, :],
                            op=ALU.mult)
            v.tensor_reduce(out=STATS[:, TT + tt:TT + tt + 1], in_=SQT[:, :],
                            axis=AX.X, op=ALU.add)
        sums, sqs = STATS[:, 0:TT], STATS[:, TT:2 * TT]
        mean = STATS[:, 2 * TT:3 * TT]
        v.drain()
        v.tensor_scalar(out=mean, in0=sums, scalar1=1.0 / D, scalar2=None,
                        op0=ALU.mult)
        v.tensor_scalar(out=sqs, in0=sqs, scalar1=1.0 / D, scalar2=None,
                        op0=ALU.mult)
        v.drain()
        v.tensor_tensor(out=sums, in0=mean, in1=mean, op=ALU.mult)
        v.drain()
        v.tensor_tensor(out=sqs, in0=sqs, in1=sums, op=ALU.subtract)
        v.drain()
        v.tensor_scalar(out=sqs, in0=sqs, scalar1=1e-5, scalar2=None,
                        op0=ALU.add)
        v.drain()
        ctx.inc(v.reciprocal(out=sums, in_=sqs), vsem)

    @block.scalar
    def _(a):
        ctx.wait(a, vsem)
        ctx.inc(a.activation(out=STATS[:, 3 * TT:4 * TT], in_=STATS[:, 0:TT],
                             func=AF.Sqrt), asem)

    for tt in range(TT):
        buf = tt % 2

        @block.vector
        def _(v, tt=tt, buf=buf):
            if tt == 0:
                ctx.wait(v, asem)
            if tt >= 2:
                ctx.wait(v, tp, tp_after[tt - 2][0])
                if quant:
                    ctx.wait(v, tpq, tp_after[tt - 2][1])
                    ctx.wait(v, asem, asem_after[tt - 2])
            mean = STATS[:, 2 * TT + tt:2 * TT + tt + 1]
            rstd = STATS[:, 3 * TT + tt:3 * TT + tt + 1]
            v.tensor_scalar(out=XN[buf][:, :], in0=XT[tt][:, :], scalar1=mean,
                            scalar2=rstd, op0=ALU.subtract, op1=ALU.mult)
            ctx.inc(v.tensor_copy(out=XNB[buf][:, :], in_=XN[buf][:, :]), vsem)

        if quant:
            @block.scalar
            def _(a, buf=buf):
                ctx.wait(a, vsem)
                ctx.inc(a.sign(out=sc["SGN"][:, 0:D], in_=XN[buf][:, :]), asem)
            asem_after[tt] = ctx.cnt[asem]

            @block.vector
            def _(v, buf=buf):
                ctx.wait(v, asem)
                last = emit_fp4_dve(v, sc, XN[buf][:, :], XQB[buf][:, :], D)
                ctx.inc(last, vsem)

        @block.sync
        def _(sync, tt=tt, buf=buf):
            ctx.wait(sync, vsem)
            for c in range(DC):
                ctx.inc(sync.dma_start_transpose(
                    out=xnT[c][:, own_off + tt * 128: own_off + (tt + 1) * 128],
                    in_=XNB[buf][:, c * 128:(c + 1) * 128]), tp, 16)

        if quant:
            @block.sync
            def _(sync, tt=tt, buf=buf):
                ctx.wait(sync, vsem)
                for c in range(DC):
                    ctx.inc(sync.dma_start_transpose(
                        out=xqT[c][:, tt * 128:(tt + 1) * 128],
                        in_=XQB[buf][:, c * 128:(c + 1) * 128]), tpq, 16)
        tp_after[tt] = (ctx.cnt[tp], ctx.cnt[tpq])

    return tp


def build_launch1():
    nc = bass.Bass("TRN2", target_bir_lowering=False)
    x_in = nc.dram_tensor("x_in", [T, D], F32, kind="ExternalInput").ap()
    xnT_out = nc.dram_tensor("xnT_own", [D, T], BF16, kind="ExternalOutput").ap()
    xqT_out = nc.dram_tensor("xqT_own", [D, T], BF16, kind="ExternalOutput").ap()
    with ExitStack() as stack:
        ctx = Ctx(nc, stack)
        block = stack.enter_context(nc.Block())
        xnT = [ctx.sb(f"xnT{c}", [128, T], BF16) for c in range(DC)]
        xqT = [ctx.sb(f"xqT{c}", [128, T], BF16) for c in range(DC)]
        tp = emit_ln_phase(ctx, block, x_in, xnT, xqT, 0, "ln0")

        @block.gpsimd
        def _(g):
            ctx.wait(g, tp)
            ctx.wait(g, "ln0_tpq")
            for c in range(DC):
                ctx.inc(g.dma_start(out=xnT_out[c * 128:(c + 1) * 128, :],
                                    in_=xnT[c][:, :]), "st", 16)
                ctx.inc(g.dma_start(out=xqT_out[c * 128:(c + 1) * 128, :],
                                    in_=xqT[c][:, :]), "st", 16)
            ctx.wait(g, "st")
    return nc


def shard_inputs(full):
    x = np.ascontiguousarray(np.asarray(full["L0_attn_x"]))
    shards = []
    for c in range(8):
        bi, hf = c // 2, c % 2
        shards.append(np.ascontiguousarray(x[bi, hf * T:(hf + 1) * T, :]))
    return shards


def emit_barrier(ctx, block, engines=("sync", "vector", "scalar", "tensor",
                                      "gpsimd"), exclude=()):
    snap = {k: v for k, v in ctx.cnt.items() if v > 0 and k not in exclude}
    for engname in engines:
        dec = getattr(block, engname)

        @dec
        def _(e, snap=snap):
            for name, val in snap.items():
                e.wait_ge(ctx.sem(name), val)


QW = 1024
PAIRS = [[0, 1], [2, 3], [4, 5], [6, 7]]
ALL8 = [[0, 1, 2, 3, 4, 5, 6, 7]]


def sub_ctx(ctx, scope):
    c = Ctx(ctx.nc, scope)
    c.sems, c.cnt = ctx.sems, ctx.cnt
    return c


def emit_wquant(ctx, block, raw_ap, R, C, dstT, sc, tag, dt_raw=F32):
    with ExitStack() as _wq_scope:
        _c = sub_ctx(ctx, _wq_scope)
        _emit_wquant(_c, block, raw_ap, R, C, dstT, sc, tag, dt_raw)


def _emit_wquant(ctx, block, raw_ap, R, C, dstT, sc, tag, dt_raw=F32):
    RC_, CC = R // 128, C // 128
    cchunks = [(c0, min(QW, C - c0)) for c0 in range(0, C, QW)]
    NCH = len(cchunks)
    WRAW = [ctx.sb(f"{tag}_WRAW{i}", [128, C], dt_raw) for i in range(2)]
    WQB = [ctx.sb(f"{tag}_WQB{i}", [128, C], BF16) for i in range(2)]
    ld, vs, as_, tp = f"{tag}_ld", f"{tag}_v", f"{tag}_a", f"{tag}_tp"
    tp2 = f"{tag}_tp2"
    tp_after, a_after, v_after = {}, {}, {}
    for ch in range(RC_):
        buf = ch % 2

        @block.sync
        def _(sync, ch=ch, buf=buf):
            if ch >= 2:
                ctx.wait(sync, vs, v_after[ch - 2])
                ctx.wait(sync, as_, a_after[ch - 2])
            ctx.inc(sync.dma_start(out=WRAW[buf][:, :],
                                   in_=raw_ap[ch * 128:(ch + 1) * 128, :]), ld, 16)

        for ci, (c0, w) in enumerate(cchunks):
            gi = ch * NCH + ci

            @block.scalar
            def _(a, ch=ch, buf=buf, c0=c0, w=w, gi=gi):
                ctx.wait(a, ld)
                if gi >= 2:
                    g2 = gi - 2
                    ctx.wait(a, vs, v_after.get(("c", g2), 0))
                ctx.inc(a.sign(out=sc["SGN"][:, (gi % 2) * QW:(gi % 2) * QW + w],
                               in_=WRAW[buf][:, c0:c0 + w]), as_)
            a_after[ch] = ctx.cnt[as_]

            @block.vector
            def _(v, ch=ch, buf=buf, c0=c0, w=w, gi=gi):
                ctx.wait(v, as_)
                if ch >= 2 and c0 == 0:
                    ctx.wait(v, tp, tp_after[ch - 2][0])
                    if tp_after[ch - 2][1]:
                        ctx.wait(v, tp2, tp_after[ch - 2][1])
                ctx.inc(emit_fp4_dve(v, sc, WRAW[buf][:, c0:c0 + w],
                                     WQB[buf][:, c0:c0 + w], w,
                                     sgn_buf=gi % 2), vs)
            v_after[("c", gi)] = ctx.cnt[vs]
        v_after[ch] = ctx.cnt[vs]

        @block.sync
        def _(sync, ch=ch, buf=buf):
            ctx.wait(sync, vs)
            for j in range(CC):
                ctx.inc(sync.dma_start_transpose(
                    out=dstT[j][:, ch * 128:(ch + 1) * 128],
                    in_=WQB[buf][:, j * 128:(j + 1) * 128]), tp, 16)
        tp_after[ch] = (ctx.cnt[tp], ctx.cnt.get(tp2, 0))

    ctx.sem(tp)
    ctx.sem(tp2)
    return tp


def build_launch2(debug=False):
    nc = bass.Bass("TRN2", target_bir_lowering=False)
    io = {}
    for name, shape, dt, kind in [
        ("x_in", [T, D], F32, "ExternalInput"),
        ("xnT_full", [D, S], BF16, "ExternalInput"),
        ("xnT_own", [D, T], BF16, "ExternalInput"),
        ("xqT_own", [D, T], BF16, "ExternalInput"),
        ("w_gate", [I0, D], F32, "ExternalInput"),
        ("w_up", [I0, D], F32, "ExternalInput"),
        ("w_down", [D, I0], F32, "ExternalInput"),
        ("wv_wT", [D, D], F32, "ExternalInput"),
        ("wv_b", [1, D], F32, "ExternalInput"),
        ("out_wT", [D, D], F32, "ExternalInput"),
        ("out_b", [1, D], F32, "ExternalInput"),
        ("watt", [NH, HD, HD], F32, "ExternalInput"),
        ("eye", [128, 128], F32, "ExternalInput"),
        ("x1_out", [T, D], F32, "ExternalOutput"),
        ("xn1T_out", [D, T], BF16, "ExternalOutput"),
        ("h_park", [T, I0], BF16, "Internal"),
        ("yffn_park", [T, D], BF16, "Internal"),
    ] + ([("o_yffn", [T, D], BF16, "ExternalOutput"),
          ("o_h", [T, I0], BF16, "ExternalOutput")] if debug else []):
        io[name] = nc.dram_tensor(name, shape, dt, kind=kind).ap()
    io["xnT_loads"] = lambda c: [(0, S, io["xnT_full"][c * 128:(c + 1) * 128, :])]
    io["dbg_park_h"] = debug
    with ExitStack() as stack:
        ctx = Ctx(nc, stack)
        block = stack.enter_context(nc.Block())
        emit_l0_body(ctx, stack, block, io)
        if debug:
            @block.gpsimd
            def _(g):
                ctx.wait(g, "stf")
                ctx.inc(g.dma_start(out=io["o_yffn"], in_=io["yffn_park"]),
                        "dbg", 16)
                ctx.inc(g.dma_start(out=io["o_h"], in_=io["h_park"]), "dbg", 16)
                ctx.wait(g, "dbg")
    return nc


def emit_l0_ffn(ctx, block, io):
    nc = ctx.nc
    NHALF = I0 // 2
    NPARTS = [(0, 512), (512, 512), (1024, 384)]
    IC0 = I0 // 128

    with ExitStack() as fs:
        cf = sub_ctx(ctx, fs)
        sc = mk_quant_scratch(cf, "q", QW)
        WGQ_T = [cf.sb(f"WGQT{i}", [128, I0], BF16) for i in range(DC)]
        WUQ_T = [cf.sb(f"WUQT{i}", [128, I0], BF16) for i in range(DC)]
        WDQ_T = [cf.sb(f"WDQT{i}", [128, D], BF16) for i in range(IC0)]
        if "wg_own" in io:
            with ExitStack() as ws:
                cw = sub_ctx(ctx, ws)
                TMPg = [cw.sb("WTMPg0", [128, I0], BF16)]
                TMPu = [cw.sb("WTMPu0", [128, I0], BF16)]
                TMPd = [cw.sb(f"WTMPd{j}", [128, 128], BF16)
                        for j in range(IC0)]
                emit_wquant(cw, block, io["wg_own"], I0, 128, TMPg, sc, "wg")

                @block.gpsimd
                def _(g):
                    ctx.wait(g, "wg_tp")
                    ctx.inc(g.dma_start(out=io["agwg_in"],
                                        in_=TMPg[0][:, :]), "wgsh", 16)
                    ctx.wait(g, "wgsh")
                    ctx.inc(g.collective_compute(
                        "AllGather", ALU.bypass, replica_groups=ALL8,
                        ins=[io["agwg_in"]], outs=[io["agwg_out"]]), "ccwg")

                emit_wquant(cw, block, io["wu_own"], I0, 128, TMPu, sc, "wu")

                @block.gpsimd
                def _(g):
                    ctx.wait(g, "wu_tp")
                    ctx.inc(g.dma_start(out=io["agwu_in"],
                                        in_=TMPu[0][:, :]), "wush", 16)
                    ctx.wait(g, "wush")
                    ctx.inc(g.collective_compute(
                        "AllGather", ALU.bypass, replica_groups=ALL8,
                        ins=[io["agwu_in"]], outs=[io["agwu_out"]]), "ccwu")

                emit_wquant(cw, block, io["wd_own"], 128, I0, TMPd, sc, "wd")

                @block.gpsimd
                def _(g):
                    ctx.wait(g, "wd_tp")
                    for j in range(IC0):
                        ctx.inc(g.dma_start(
                            out=io["agwd_in"][j * 128:(j + 1) * 128, :],
                            in_=TMPd[j][:, :]), "wdsh", 16)
                    ctx.wait(g, "wdsh")
                    ctx.inc(g.collective_compute(
                        "AllGather", ALU.bypass, replica_groups=ALL8,
                        ins=[io["agwd_in"]], outs=[io["agwd_out"]]), "ccwd")
                    ctx.wait(g, "ccwg")
                    for c in range(DC):
                        ctx.inc(g.dma_start(
                            out=WGQ_T[c][:, :],
                            in_=io["agwg_out"][c * 128:(c + 1) * 128, :]),
                            "wfin", 16)
                    ctx.wait(g, "ccwu")
                    for c in range(DC):
                        ctx.inc(g.dma_start(
                            out=WUQ_T[c][:, :],
                            in_=io["agwu_out"][c * 128:(c + 1) * 128, :]),
                            "wfin", 16)
                    ctx.wait(g, "ccwd")
                    for j in range(IC0):
                        for m in range(DC):
                            ctx.inc(g.dma_start(
                                out=WDQ_T[j][:, m * 128:(m + 1) * 128],
                                in_=io["agwd_out"][m * I0 + j * 128:
                                                   m * I0 + (j + 1) * 128, :]),
                                "wfin", 16)
        else:
            emit_wquant(cf, block, io["w_gate"], I0, D, WGQ_T, sc, "wg")
            emit_wquant(cf, block, io["w_up"], I0, D, WUQ_T, sc, "wu")
            emit_wquant(cf, block, io["w_down"], D, I0, WDQ_T, sc, "wd")
        emit_barrier(ctx, block)

        with ExitStack() as p3s:
            c3 = sub_ctx(ctx, p3s)
            XQT = [c3.sb(f"XQT{i}", [128, T], BF16) for i in range(DC)]
            GSIL = c3.sb("GSIL", [128, NHALF], BF16)
            HT = [c3.sb(f"HT{i}", [128, I0], BF16) for i in range(2)]
            HQT = [c3.sb(f"HQT{i}", [128, I0], BF16) for i in range(1)]
            YF = [c3.sb(f"YF{i}", [128, D], BF16) for i in range(2)]
            PG = c3.ps("PG", [128, NHALF])
            PU = c3.ps("PU", [128, NHALF])
            PY = c3.ps("PY", [128, D])

            @block.sync
            def _(sync):
                for c in range(DC):
                    ctx.inc(sync.dma_start(
                        out=XQT[c][:, :],
                        in_=io["xqT_own"][c * 128:(c + 1) * 128, :]), "p3ld", 16)

            cch = [(c0, min(QW, I0 - c0)) for c0 in range(0, I0, QW)]
            vq_after, aq_after, tpq_after, dn_after, yst_after = {}, {}, {}, {}, {}
            for tt in range(TT):
                buf = tt % 2
                for nh in range(2):
                    @block.tensor
                    def _(tn, tt=tt, nh=nh):
                        if tt == 0 and nh == 0:
                            ctx.wait(tn, "p3ld")
                        ctx.wait(tn, "p3a")
                        ctx.wait(tn, "p3v")
                        for k in range(DC):
                            lhs = XQT[k][:, tt * 128:(tt + 1) * 128]
                            for (n0, nn) in NPARTS:
                                tn.matmul(out=PG[:, n0:n0 + nn], lhsT=lhs,
                                          rhs=WGQ_T[k][:, nh * NHALF + n0:
                                                       nh * NHALF + n0 + nn],
                                          start=(k == 0), stop=(k == DC - 1))
                            for (n0, nn) in NPARTS:
                                ins = tn.matmul(
                                    out=PU[:, n0:n0 + nn], lhsT=lhs,
                                    rhs=WUQ_T[k][:, nh * NHALF + n0:
                                                 nh * NHALF + n0 + nn],
                                    start=(k == 0), stop=(k == DC - 1))
                        ctx.inc(ins, "p3mm")

                    @block.scalar
                    def _(a, tt=tt, nh=nh):
                        ctx.wait(a, "p3mm")
                        ctx.wait(a, "p3v")
                        ctx.inc(a.activation(out=GSIL[:, :], in_=PG[:, :],
                                             func=AF.Silu), "p3a")

                    @block.vector
                    def _(v, tt=tt, nh=nh, buf=buf):
                        ctx.wait(v, "p3a")
                        if tt >= 2 and nh == 0:
                            ctx.wait(v, "p3tp", tpq_after[tt - 2])
                        ctx.inc(v.tensor_tensor(
                            out=HT[buf][:, nh * NHALF:(nh + 1) * NHALF],
                            in0=PU[:, :], in1=GSIL[:, :], op=ALU.mult), "p3v")

                for ci, (c0, w) in enumerate(cch):
                    gi = tt * len(cch) + ci

                    @block.scalar
                    def _(a, tt=tt, buf=buf, c0=c0, w=w, gi=gi):
                        ctx.wait(a, "p3v")
                        if gi >= 2:
                            ctx.wait(a, "p3q", vq_after.get(("c", gi - 2), 0))
                        ctx.inc(a.sign(
                            out=sc["SGN"][:, (gi % 2) * QW:(gi % 2) * QW + w],
                            in_=HT[buf][:, c0:c0 + w]), "p3sa")

                    @block.vector
                    def _(v, tt=tt, buf=buf, c0=c0, w=w, gi=gi):
                        ctx.wait(v, "p3sa")
                        ctx.inc(emit_fp4_dve(v, sc, HT[buf][:, c0:c0 + w],
                                             HT[buf][:, c0:c0 + w], w,
                                             sgn_buf=gi % 2), "p3q")
                    vq_after[("c", gi)] = ctx.cnt["p3q"]
                vq_after[tt] = ctx.cnt["p3q"]

                if io.get("dbg_park_h"):
                    @block.gpsimd
                    def _(g, tt=tt, buf=buf):
                        ctx.wait(g, "p3q")
                        ctx.inc(g.dma_start(
                            out=io["h_park"][tt * 128:(tt + 1) * 128, :],
                            in_=HT[buf][:, :]), "p3hdbg", 16)

                @block.sync
                def _(sync, tt=tt, buf=buf):
                    ctx.wait(sync, "p3q")
                    if tt >= 1:
                        ctx.wait(sync, "p3dn", dn_after[tt - 1])
                    for hc in range(IC0):
                        ctx.inc(sync.dma_start_transpose(
                            out=HQT[0][:, hc * 128:(hc + 1) * 128],
                            in_=HT[buf][:, hc * 128:(hc + 1) * 128]), "p3tp", 16)
                tpq_after[tt] = ctx.cnt["p3tp"]

                @block.tensor
                def _(tn, tt=tt, buf=buf):
                    ctx.wait(tn, "p3tp")
                    ctx.wait(tn, "p3yv")
                    for hc in range(IC0):
                        lhs = HQT[0][:, hc * 128:(hc + 1) * 128]
                        for n0 in (0, 512):
                            ins = tn.matmul(out=PY[:, n0:n0 + 512], lhsT=lhs,
                                            rhs=WDQ_T[hc][:, n0:n0 + 512],
                                            start=(hc == 0), stop=(hc == IC0 - 1))
                    ctx.inc(ins, "p3dn")
                dn_after[tt] = ctx.cnt["p3dn"]

                @block.vector
                def _(v, tt=tt, buf=buf):
                    ctx.wait(v, "p3dn")
                    if tt >= 2:
                        ctx.wait(v, "p3yst", yst_after[tt - 2])
                    ctx.inc(v.tensor_copy(out=YF[buf][:, :], in_=PY[:, :]),
                            "p3yv")

                @block.gpsimd
                def _(g, tt=tt, buf=buf):
                    ctx.wait(g, "p3yv")
                    ctx.inc(g.dma_start(
                        out=io["yffn_park"][tt * 128:(tt + 1) * 128, :],
                        in_=YF[buf][:, :]), "p3yst", 16)
                yst_after[tt] = ctx.cnt["p3yst"]

            emit_barrier(ctx, block)


def emit_l0_body(ctx, stack, block, io):
    nc = ctx.nc
    emit_l0_ffn(ctx, block, io)
    emit_l0_attn_section(ctx, block, io)
    emit_ln1_out(ctx, block, io)


def emit_l0_attn_section(ctx, block, io):
    nc = ctx.nc

    with ExitStack() as As:
        ca = sub_ctx(ctx, As)
        xnT = [ca.sb(f"xnT{c}", [128, S], BF16) for c in range(DC)]
        VT = [ca.sb(f"VT{i}", [128, NH * 65], BF16) for i in range(ST)]
        ATTNT = [ca.sb(f"ATTNT{c}", [128, T], BF16) for c in range(DC)]
        WATT = ca.sb("WATT", [128, NH * HD], BF16)
        EYE32 = ca.sb("EYE32", [128, 128])
        CBIAS = ca.sb("CBIAS0", [128, 1])

        with ExitStack() as p5s:
            c5 = sub_ctx(ctx, p5s)
            WVT = [c5.sb(f"WVT{i}", [128, D], BF16) for i in range(DC)]
            BIAS_T = c5.sb("BIAS_T", [128, D])
            WVBR = c5.sb("WVBR", [1, D])
            ONESF = c5.sb("ONESF", [1, 128])
            VROW = [c5.sb(f"VROW{i}", [128, D], BF16) for i in range(2)]
            PVX = c5.ps("PVX", [128, D])
            PBI = c5.ps("PBI", [128, D])

            @block.sync
            def _(sync):
                if "cc0" in ctx.cnt:
                    ctx.wait(sync, "cc0")
                for c in range(DC):
                    for (c0, c1, src) in io["xnT_loads"](c):
                        ctx.inc(sync.dma_start(out=xnT[c][:, c0:c1], in_=src),
                                "p5ld", 16)
                ctx.inc(sync.dma_start(out=WVBR[:, :], in_=io["wv_b"]), "p5ld", 16)
                ctx.inc(sync.dma_start(out=EYE32[:, :], in_=io["eye"]), "p5ld", 16)

            @block.gpsimd
            def _(g):
                for c in range(DC):
                    ctx.inc(g.dma_start(out=WVT[c][:, :],
                                        in_=io["wv_wT"][c * 128:(c + 1) * 128, :]),
                            "p5lg", 16)
                ctx.inc(g.dma_start(
                    out=WATT[0:64, :].rearrange("d (h e) -> d h e", h=NH),
                    in_=io["watt"].transpose([1, 0, 2])), "p5lg", 16)
                ctx.inc(g.dma_start(
                    out=WATT[64:128, :].rearrange("d (h e) -> d h e", h=NH),
                    in_=io["watt"].transpose([1, 0, 2])), "p5lg", 16)
                for st_ in range(ST):
                    ones_col = VT[st_][:, :].rearrange(
                        "p (h c) -> p h c", c=65)[:, :, 64:65]
                    ctx.inc(g.memset(ones_col, 1.0), "p5o")

            @block.vector
            def _(v):
                v.memset(CBIAS[:, :], -L0_SHIFT)
                ctx.inc(v.memset(ONESF[:, :], 1.0), "p5v")

            @block.tensor
            def _(tn):
                ctx.wait(tn, "p5ld")
                ctx.wait(tn, "p5lg")
                ctx.wait(tn, "p5v")
                for n0 in (0, 512):
                    ins = tn.matmul(out=PBI[:, n0:n0 + 512], lhsT=ONESF[:, :],
                                    rhs=WVBR[:, n0:n0 + 512], start=True,
                                    stop=True)
                ctx.inc(ins, "p5mm")

            @block.vector
            def _(v):
                ctx.wait(v, "p5mm")
                ctx.inc(v.tensor_copy(out=BIAS_T[:, :], in_=PBI[:, :]), "p5v")

            vrow_after = {}
            for st_ in range(ST):
                buf = st_ % 2

                @block.tensor
                def _(tn, st_=st_):
                    ctx.wait(tn, "p5v")
                    if st_ >= 1:
                        ctx.wait(tn, "p5ev", vrow_after[st_ - 1])
                    for c in range(DC):
                        lhs = xnT[c][:, st_ * 128:(st_ + 1) * 128]
                        for n0 in (0, 512):
                            ins = tn.matmul(out=PVX[:, n0:n0 + 512], lhsT=lhs,
                                            rhs=WVT[c][:, n0:n0 + 512],
                                            start=(c == 0), stop=(c == DC - 1))
                    ctx.inc(ins, "p5mm")

                @block.vector
                def _(v, st_=st_, buf=buf):
                    ctx.wait(v, "p5mm")
                    if st_ >= 2:
                        ctx.wait(v, "p5pc", NH * (st_ - 1))
                    ctx.inc(v.scalar_tensor_tensor(
                        out=VROW[buf][:, :], in0=PVX[:, :], scalar=1.0,
                        in1=BIAS_T[:, :], op0=ALU.mult, op1=ALU.add), "p5ev")

                @block.gpsimd
                def _(g, st_=st_, buf=buf):
                    ctx.wait(g, "p5ev")
                    for h in range(NH):
                        ctx.inc(g.tensor_copy(
                            out=VT[st_][:, 65 * h:65 * h + 64],
                            in_=VROW[buf][:, 64 * h:64 * h + 64]), "p5pc", 1)
                vrow_after[st_] = ctx.cnt["p5ev"]

            emit_barrier(ctx, block)

        emit_attn_pipelined(ctx, block, dict(
            mode="l0", tag="A0", ATTNT=ATTNT, EYE32=EYE32,
            keys_lhsT=lambda h, kc: xnT[h // 2][
                64 * (h % 2):64 * (h % 2) + 64, kc * 128:(kc + 1) * 128],
            v_lhsT=lambda kc, h: VT[kc][:, 65 * h:65 * h + 65],
            exp_bias=lambda h, kc: CBIAS[:, 0:1],
            qb_base=lambda h: 64 * (h % 2),
            qsrc_dram=lambda c, qc: io["xnT_own"][
                c * 128:(c + 1) * 128, qc * 1024:(qc + 1) * 1024],
            watt_lhsT=lambda h: WATT[64 * (h % 2):64 * (h % 2) + 64,
                                     64 * h:64 * h + 64],
        ))

        with ExitStack() as p8s:
            c8 = sub_ctx(ctx, p8s)
            OUTWT = [c8.sb(f"OUTWT{i}", [128, D], BF16) for i in range(DC)]
            OUTBR = c8.sb("OUTBR", [1, D])
            ONES8 = c8.sb("ONES8", [1, 128])
            XTL = [c8.sb(f"XTL{i}", [128, D]) for i in range(2)]
            YFB = [c8.sb(f"YFB{i}", [128, D], BF16) for i in range(2)]
            X1TL = [c8.sb(f"X1TL{i}", [128, D]) for i in range(2)]
            PYA = c8.ps("PYA", [128, D])

            @block.sync
            def _(sync):
                ctx.inc(sync.dma_start(out=OUTBR[:, :], in_=io["out_b"]),
                        "p8ld", 16)

            @block.gpsimd
            def _(g):
                for c in range(DC):
                    ctx.inc(g.dma_start(
                        out=OUTWT[c][:, :],
                        in_=io["out_wT"][c * 128:(c + 1) * 128, :]), "p8lg", 16)

            @block.vector
            def _(v):
                ctx.inc(v.memset(ONES8[:, :], 1.0), "p8v")

            x1_after, st_after = {}, {}
            for tt in range(TT):
                buf = tt % 2

                @block.sync
                def _(sync, tt=tt, buf=buf):
                    if tt >= 2:
                        ctx.wait(sync, "p8v", x1_after[tt - 2])
                    ctx.inc(sync.dma_start(
                        out=XTL[buf][:, :],
                        in_=io["x_in"][tt * 128:(tt + 1) * 128, :]), "p8xl", 16)
                    ctx.inc(sync.dma_start(
                        out=YFB[buf][:, :],
                        in_=io["yffn_park"][tt * 128:(tt + 1) * 128, :]),
                        "p8xl", 16)

                @block.tensor
                def _(tn, tt=tt):
                    ctx.wait(tn, "p8ld")
                    ctx.wait(tn, "p8lg")
                    ctx.wait(tn, "p8v")
                    for n0 in (0, 512):
                        tn.matmul(out=PYA[:, n0:n0 + 512], lhsT=ONES8[:, :],
                                  rhs=OUTBR[:, n0:n0 + 512], start=True,
                                  stop=False)
                    for c in range(DC):
                        lhs = ATTNT[c][:, tt * 128:(tt + 1) * 128]
                        for n0 in (0, 512):
                            ins = tn.matmul(out=PYA[:, n0:n0 + 512], lhsT=lhs,
                                            rhs=OUTWT[c][:, n0:n0 + 512],
                                            start=False, stop=(c == DC - 1))
                    ctx.inc(ins, "p8mm")

                @block.vector
                def _(v, tt=tt, buf=buf):
                    ctx.wait(v, "p8mm")
                    ctx.wait(v, "p8xl")
                    if tt >= 2:
                        ctx.wait(v, "p8st", st_after[tt - 2])
                    v.scalar_tensor_tensor(
                        out=X1TL[buf][:, :], in0=PYA[:, :], scalar=1.0,
                        in1=XTL[buf][:, :], op0=ALU.mult, op1=ALU.add)
                    ctx.inc(v.tensor_tensor(out=X1TL[buf][:, :],
                                            in0=X1TL[buf][:, :],
                                            in1=YFB[buf][:, :], op=ALU.add),
                            "p8v")
                x1_after[tt] = ctx.cnt["p8v"]

                @block.gpsimd
                def _(g, tt=tt, buf=buf):
                    ctx.wait(g, "p8v")
                    ctx.inc(g.dma_start(
                        out=io["x1_out"][tt * 128:(tt + 1) * 128, :],
                        in_=X1TL[buf][:, :]), "p8st", 16)
                st_after[tt] = ctx.cnt["p8st"]

            emit_barrier(ctx, block)


def emit_ln1_out(ctx, block, io):
    xn1T = [ctx.sb(f"xn1T{c}", [128, T], BF16) for c in range(DC)]
    tp = emit_ln_phase(ctx, block, io["x1_out"], xn1T, None, 0, "ln1")

    @block.gpsimd
    def _(g):
        ctx.wait(g, tp)
        for c in range(DC):
            ctx.inc(g.dma_start(out=io["xn1T_out"][c * 128:(c + 1) * 128, :],
                                in_=xn1T[c][:, :]), "stf", 16)
        ctx.wait(g, "stf")


def emit_attn_pipelined(ctx, block, cfg):
    nc = ctx.nc
    mode = cfg["mode"]
    tag = cfg["tag"]
    ATTNT = cfg["ATTNT"]
    EYE32 = cfg["EYE32"]
    NKC = ST

    sqk, spv, sa = f"{tag}qk", f"{tag}pv", f"{tag}a"
    sv_, stp, smq = f"{tag}v", f"{tag}tp", f"{tag}q"
    sql, src_ = f"{tag}ql", f"{tag}rc"

    for qc in range(2):
        with ExitStack() as qs:
            cq = sub_ctx(ctx, qs)
            Q0T = ([cq.sb(f"{tag}Q0T{qc}_{i}", [128, 1024], BF16)
                    for i in range(DC)] if mode == "l0" else None)
            for name in (sqk, spv, sa, sv_, stp, smq, sql, src_):
                ctx.sem(name)

            if mode == "l0":
                with ExitStack() as qps:
                    cqp = sub_ctx(ctx, qps)
                    QS_ALL = [cqp.sb(f"{tag}QS{qc}_{i}", [128, 1024], BF16)
                              for i in range(DC)]
                    PS_QP = [cqp.ps(f"{tag}QP{qc}_{i}", [64, 1024])
                             for i in range(2)]

                    @block.sync
                    def _(sync, qc=qc):
                        for c in range(DC):
                            ctx.inc(sync.dma_start(
                                out=QS_ALL[c][:, :],
                                in_=cfg["qsrc_dram"](c, qc)), sql, 16)
                    bQL0 = ctx.cnt[sql]
                    bQ0 = ctx.cnt[smq]
                    bV0 = ctx.cnt[sv_]

                    @block.tensor
                    def _(tn, qc=qc):
                        tn.wait_ge(ctx.sem(sql), bQL0)
                        for h in range(NH):
                            if h >= 2:
                                tn.wait_ge(ctx.sem(sv_), bV0 + h - 1)
                            lhsw = cfg["watt_lhsT"](h)
                            rhsq = QS_ALL[h // 2][64 * (h % 2):64 * (h % 2) + 64, :]
                            for n0 in (0, 512):
                                ins = tn.matmul(out=PS_QP[h % 2][:, n0:n0 + 512],
                                                lhsT=lhsw,
                                                rhs=rhsq[:, n0:n0 + 512],
                                                start=True, stop=True)
                            ctx.inc(ins, smq)

                    @block.vector
                    def _(v, qc=qc):
                        for h in range(NH):
                            v.wait_ge(ctx.sem(smq), bQ0 + h + 1)
                            ctx.inc(v.tensor_scalar(
                                out=Q0T[h // 2][64 * (h % 2):64 * (h % 2) + 64, :],
                                in0=PS_QP[h % 2][:, :], scalar1=0.125,
                                scalar2=None, op0=ALU.mult), sv_)
                    emit_barrier(ctx, block, engines=("tensor", "vector",
                                                      "scalar", "sync"))

            ATTN = [cq.sb(f"{tag}AT{qc}_{i}", [128, D], BF16) for i in range(8)]
            QB = (cq.sb(f"{tag}QB{qc}", [128, 1024], BF16)
                  if mode == "l1" else None)
            PT = [cq.sb(f"{tag}PT{qc}_{i}", [128, 1024], BF16) for i in range(2)]
            OT_SB = cq.sb(f"{tag}OT{qc}", [128, 1024])
            RECS = cq.sb(f"{tag}RC{qc}", [128, 1])
            PS_S = [cq.ps(f"{tag}SS{qc}_{i}", [128, 1024]) for i in range(2)]
            PS_O = cq.ps(f"{tag}SO{qc}", [128, 1024])
            PS_T = [cq.ps(f"{tag}ST{qc}_{i}", [128, 128]) for i in range(2)]

            bQK, bPV, bA = ctx.cnt[sqk], ctx.cnt[spv], ctx.cnt[sa]
            bV, bTP = ctx.cnt[sv_], ctx.cnt[stp]

            vec_events = []
            for h in range(NH):
                if mode == "l1":
                    vec_events += [("qcopy", h)]
                if h >= 1:
                    vec_events += [("evac", h - 1)]
                    vec_events += [("mult", h - 1, j) for j in range(8)]
            vec_events += [("evac", NH - 1)]
            vec_events += [("mult", NH - 1, j) for j in range(8)]
            VC = {ev: bV + i + 1 for i, ev in enumerate(vec_events)}

            def c_exp(g):
                return bA + g + 1

            def c_qk(g):
                return bQK + g + 1

            def c_pv(g):
                return bPV + g + 1

            def c_tp(h, j):
                return bTP + 8 * h + j + 1

            def q_rhs(h):
                if mode == "l0":
                    return Q0T[h // 2][64 * (h % 2):64 * (h % 2) + 64, :]
                qb0 = cfg["qb_base"](h)
                return QB[qb0:qb0 + 64, :]

            def tensor_head(tn, h):
                def emit_pv(g):
                    h_, kc_ = g // NKC, g % NKC
                    tn.wait_ge(ctx.sem(sa), c_exp(g))
                    if kc_ == 0 and h_ >= 1:
                        tn.wait_ge(ctx.sem(sv_), VC[("evac", h_ - 1)])
                    for n0 in (0, 512):
                        ins = tn.matmul(out=PS_O[0:65, n0:n0 + 512],
                                        lhsT=cfg["v_lhsT"](kc_, h_),
                                        rhs=PT[g % 2][:, n0:n0 + 512],
                                        start=(kc_ == 0), stop=(kc_ == NKC - 1))
                    ctx.inc(ins, spv)

                def emit_transposes(hh):
                    for j in range(8):
                        tn.wait_ge(ctx.sem(sv_), VC[("evac", hh)])
                        gj = 8 * hh + j - 2
                        if gj >= 0:
                            tn.wait_ge(ctx.sem(sv_),
                                       VC[("mult", gj // 8, gj % 8)])
                        ins = tn.transpose(out=PS_T[j % 2][:, 0:65],
                                           in_=OT_SB[0:65, j * 128:(j + 1) * 128],
                                           identity=EYE32[0:65, 0:65])
                        ctx.inc(ins, stp)

                rq = q_rhs(h)
                for kc in range(NKC):
                    g = NKC * h + kc
                    if kc == 0 and mode == "l1":
                        tn.wait_ge(ctx.sem(sv_), VC[("qcopy", h)])
                    if g >= 2:
                        tn.wait_ge(ctx.sem(sa), c_exp(g - 2))
                    lhs = cfg["keys_lhsT"](h, kc)
                    for n0 in (0, 512):
                        ins = tn.matmul(out=PS_S[g % 2][:, n0:n0 + 512],
                                        lhsT=lhs, rhs=rq[:, n0:n0 + 512],
                                        start=True, stop=True)
                    ctx.inc(ins, sqk)
                    if g >= 1:
                        emit_pv(g - 1)
                    if kc == 4 and h >= 1:
                        emit_transposes(h - 1)
                if h == NH - 1:
                    emit_pv(NKC * NH - 1)
                    emit_transposes(NH - 1)

            for h in range(NH):
                @block.tensor
                def _(tn, h=h):
                    tensor_head(tn, h)

            for g0 in range(0, NKC * NH, 128):
                @block.scalar
                def _(a, g0=g0):
                    for g in range(g0, g0 + 128):
                        h_, kc_ = g // NKC, g % NKC
                        a.wait_ge(ctx.sem(sqk), c_qk(g))
                        if g >= 2:
                            a.wait_ge(ctx.sem(spv), c_pv(g - 2))
                        ctx.inc(a.activation(out=PT[g % 2][:, :],
                                             in_=PS_S[g % 2][:, :], func=AF.Exp,
                                             bias=cfg["exp_bias"](h_, kc_),
                                             scale=1.0), sa)

            def vector_round(v, h, last=False):
                def emit_qprep(hh):
                    qb0 = cfg["qb_base"](hh)
                    if hh >= 1:
                        v.wait_ge(ctx.sem(sqk), bQK + NKC * hh)
                    ctx.inc(v.tensor_copy(out=QB[qb0:qb0 + 64, :],
                                          in_=cfg["q_sb"](hh, qc)), sv_)

                def emit_norm(hh):
                    v.wait_ge(ctx.sem(spv), c_pv(NKC * hh + NKC - 1))
                    if hh >= 1:
                        v.wait_ge(ctx.sem(stp), c_tp(hh - 1, 7))
                    ctx.inc(v.tensor_copy(out=OT_SB[0:65, :],
                                          in_=PS_O[0:65, :]), sv_)
                    for j in range(8):
                        v.wait_ge(ctx.sem(stp), c_tp(hh, j))
                        sl = PS_T[j % 2]
                        ctx.inc(v.reciprocal(out=RECS[:, 0:1],
                                             in_=sl[:, 64:65]), src_)
                        v.wait_ge(ctx.sem(src_), ctx.cnt[src_])
                        ctx.inc(v.tensor_scalar(
                            out=ATTN[j][:, 64 * h if False else 64 * hh:
                                        64 * hh + 64],
                            in0=sl[:, 0:64], scalar1=RECS[:, 0:1],
                            scalar2=None, op0=ALU.mult), sv_)

                if not last:
                    if mode == "l1":
                        emit_qprep(h)
                    if h >= 1:
                        emit_norm(h - 1)
                else:
                    emit_norm(NH - 1)

            for h in range(NH):
                @block.vector
                def _(v, h=h):
                    vector_round(v, h)

            @block.vector
            def _(v):
                vector_round(v, NH - 1, last=True)

            @block.sync
            def _(sync, qc=qc):
                for j in range(8):
                    t = qc * 8 + j
                    sync.wait_ge(ctx.sem(sv_), VC[("mult", NH - 1, j)])
                    for c in range(DC):
                        ctx.inc(sync.dma_start_transpose(
                            out=ATTNT[c][:, t * 128:(t + 1) * 128],
                            in_=ATTN[j][:, c * 128:(c + 1) * 128]), f"{tag}td", 16)

            emit_barrier(ctx, block)


def build_launch3():
    nc = bass.Bass("TRN2", target_bir_lowering=False)
    io = {}
    for name, shape, dt, kind in [
        ("x1_in", [T, D], F32, "ExternalInput"),
        ("xn1T_full", [D, S], BF16, "ExternalInput"),
        ("xn1T_own", [D, T], BF16, "ExternalInput"),
        ("Wq", [D, D], F32, "ExternalInput"),
        ("Wkv", [D, 512], F32, "ExternalInput"),
        ("Wo", [D, D], F32, "ExternalInput"),
        ("wgT", [I1, D], I32, "ExternalInput"),
        ("wuT", [I1, D], I32, "ExternalInput"),
        ("wdT", [I1, D], I32, "ExternalInput"),
        ("alibi", [128, NH * ST], F32, "ExternalInput"),
        ("eye", [128, 128], F32, "ExternalInput"),
        ("x2_out", [T, D], F32, "ExternalOutput"),
        ("yffn_park", [T, D], BF16, "Internal"),
    ]:
        io[name] = nc.dram_tensor(name, shape, dt, kind=kind).ap()
    io["xn1T_loads"] = lambda c: [(0, S, io["xn1T_full"][c * 128:(c + 1) * 128, :])]
    with ExitStack() as stack:
        ctx = Ctx(nc, stack)
        block = stack.enter_context(nc.Block())
        emit_l1_body(ctx, stack, block, io)
    return nc


def emit_l1_body(ctx, stack, block, io):
    nc = ctx.nc
    IC1 = I1 // 128

    with ExitStack() as As:
        ca = sub_ctx(ctx, As)
        K1T = [ca.sb(f"K1T{i}", [128, S], BF16) for i in range(2)]
        V1 = [ca.sb(f"V1_{i}", [128, NKV * 65], BF16) for i in range(ST)]
        ATTNT = [ca.sb(f"L1ATTNT{c}", [128, T], BF16) for c in range(DC)]
        EYE32 = ca.sb("L1EYE", [128, 128])
        ALIBI = ca.sb("ALIBI", [128, NH * ST])
        Q1T = [ca.sb(f"Q1T{i}", [128, T], BF16) for i in range(DC)]

        with ExitStack() as pk:
            ck = sub_ctx(ctx, pk)
            xn1T = [ck.sb(f"xn1T{c}", [128, S], BF16) for c in range(DC)]
            WKV = [ck.sb(f"WKV{c}", [128, 512], BF16) for c in range(DC)]
            KROW = [ck.sb(f"KROW{i}", [128, 256], BF16) for i in range(2)]
            VROW = [ck.sb(f"VROW1_{i}", [128, 256], BF16) for i in range(2)]
            PKV = ck.ps("PKV", [128, 512])

            @block.sync
            def _(sync):
                if "cc1" in ctx.cnt:
                    ctx.wait(sync, "cc1")
                for c in range(DC):
                    for (c0, c1, src) in io["xn1T_loads"](c):
                        ctx.inc(sync.dma_start(out=xn1T[c][:, c0:c1], in_=src),
                                "k1ld", 16)
                ctx.inc(sync.dma_start(out=EYE32[:, :], in_=io["eye"]),
                        "k1ld", 16)
                ctx.inc(sync.dma_start(out=ALIBI[:, :], in_=io["alibi"]),
                        "k1ld", 16)

            @block.gpsimd
            def _(g):
                for c in range(DC):
                    ctx.inc(g.dma_start(out=WKV[c][:, :],
                                        in_=io["Wkv"][c * 128:(c + 1) * 128, :]),
                            "k1lg", 16)
                for st_ in range(ST):
                    ones_col = V1[st_][:, :].rearrange(
                        "p (h c) -> p h c", c=65)[:, :, 64:65]
                    ctx.inc(g.memset(ones_col, 1.0), "k1o")

            krow_after = {}
            for st_ in range(ST):
                buf = st_ % 2

                @block.tensor
                def _(tn, st_=st_):
                    ctx.wait(tn, "k1ld")
                    ctx.wait(tn, "k1lg")
                    if st_ >= 2:
                        ctx.wait(tn, "k1tp", krow_after[st_ - 2][0])
                        ctx.wait(tn, "k1pc", krow_after[st_ - 2][1])
                    ctx.wait(tn, "k1v")
                    for c in range(DC):
                        ins = tn.matmul(out=PKV[:, :],
                                        lhsT=xn1T[c][:, st_ * 128:(st_ + 1) * 128],
                                        rhs=WKV[c][:, :],
                                        start=(c == 0), stop=(c == DC - 1))
                    ctx.inc(ins, "k1mm")

                @block.vector
                def _(v, st_=st_, buf=buf):
                    ctx.wait(v, "k1mm")
                    v.tensor_copy(out=KROW[buf][:, :], in_=PKV[:, 0:256])
                    ctx.inc(v.tensor_copy(out=VROW[buf][:, :],
                                          in_=PKV[:, 256:512]), "k1v")

                @block.sync
                def _(sync, st_=st_, buf=buf):
                    ctx.wait(sync, "k1v")
                    for half in range(2):
                        ctx.inc(sync.dma_start_transpose(
                            out=K1T[half][:, st_ * 128:(st_ + 1) * 128],
                            in_=KROW[buf][:, half * 128:(half + 1) * 128]),
                            "k1tp", 16)

                @block.gpsimd
                def _(g, st_=st_, buf=buf):
                    ctx.wait(g, "k1v")
                    for kv in range(NKV):
                        ctx.inc(g.tensor_copy(
                            out=V1[st_][:, 65 * kv:65 * kv + 64],
                            in_=VROW[buf][:, 64 * kv:64 * kv + 64]), "k1pc", 1)
                krow_after[st_] = (ctx.cnt["k1tp"], ctx.cnt["k1pc"])

            emit_barrier(ctx, block)

        with ExitStack() as pq:
            cqx = sub_ctx(ctx, pq)
            XO = [cqx.sb(f"XOQ{c}", [128, T], BF16) for c in range(DC)]
            WQSB = [cqx.sb(f"WQSB{c}", [128, D], BF16) for c in range(DC)]
            PQ1 = cqx.ps("PQ1", [128, 1024])

            @block.sync
            def _(sync):
                for c in range(DC):
                    ctx.inc(sync.dma_start(
                        out=XO[c][:, :],
                        in_=io["xn1T_own"][c * 128:(c + 1) * 128, :]), "q1ld", 16)

            @block.gpsimd
            def _(g):
                for c in range(DC):
                    ctx.inc(g.dma_start(out=WQSB[c][:, :],
                                        in_=io["Wq"][c * 128:(c + 1) * 128, :]),
                            "q1lg", 16)

            for oc in range(DC):
                for th in range(2):
                    @block.tensor
                    def _(tn, oc=oc, th=th):
                        ctx.wait(tn, "q1ld")
                        ctx.wait(tn, "q1v")
                        for ic in range(DC):
                            for n0 in (0, 512):
                                ins = tn.matmul(
                                    out=PQ1[:, n0:n0 + 512],
                                    lhsT=WQSB[ic][:, oc * 128:(oc + 1) * 128],
                                    rhs=XO[ic][:, th * 1024 + n0:
                                               th * 1024 + n0 + 512],
                                    start=(ic == 0), stop=(ic == DC - 1))
                        ctx.inc(ins, "q1mm")

                    @block.vector
                    def _(v, oc=oc, th=th):
                        ctx.wait(v, "q1mm")
                        ctx.inc(v.tensor_scalar(
                            out=Q1T[oc][:, th * 1024:(th + 1) * 1024],
                            in0=PQ1[:, :], scalar1=0.125, scalar2=None,
                            op0=ALU.mult), "q1v")
            emit_barrier(ctx, block)

        for th in range(2):
            with ExitStack() as ph:
                chh = sub_ctx(ctx, ph)
                HT1 = [chh.sb(f"HT1_{th}_{i}", [128, 1024], BF16)
                       for i in range(IC1)]

                with ExitStack() as pg:
                    cg = sub_ctx(ctx, pg)
                    XO2 = [cg.sb(f"XOF{th}_{c}", [128, 1024], BF16)
                           for c in range(DC)]
                    WGN = [cg.sb(f"WGN{th}_{i}", [128, D], BF16) for i in range(2)]
                    WUN = [cg.sb(f"WUN{th}_{i}", [128, D], BF16) for i in range(2)]
                    SIL1 = cg.sb(f"SIL1_{th}", [128, 1024], BF16)
                    PG1 = cg.ps(f"PG1_{th}", [128, 1024])
                    PU1 = cg.ps(f"PU1_{th}", [128, 1024])

                    @block.sync
                    def _(sync, th=th):
                        for c in range(DC):
                            ctx.inc(sync.dma_start(
                                out=XO2[c][:, :],
                                in_=io["xn1T_own"][c * 128:(c + 1) * 128,
                                                   th * 1024:(th + 1) * 1024]),
                                "f1ld", 16)

                    gu_after = {}
                    for nt in range(IC1):
                        buf = nt % 2

                        @block.gpsimd
                        def _(g, nt=nt, buf=buf, th=th):
                            if nt >= 2:
                                ctx.wait(g, "f1mm", gu_after[nt - 2])
                            ctx.inc(g.dma_start(
                                out=WGN[buf][:, :],
                                in_=io["wgT"][nt * 128:(nt + 1) * 128, :]),
                                "f1wl", 16)
                            ctx.inc(g.dma_start(
                                out=WUN[buf][:, :],
                                in_=io["wuT"][nt * 128:(nt + 1) * 128, :]),
                                "f1wl", 16)

                        @block.tensor
                        def _(tn, nt=nt, buf=buf):
                            ctx.wait(tn, "f1ld")
                            ctx.wait(tn, "f1wl")
                            ctx.wait(tn, "f1a")
                            ctx.wait(tn, "f1v")
                            for ic in range(DC):
                                for n0 in (0, 512):
                                    tn.matmul(
                                        out=PG1[:, n0:n0 + 512],
                                        lhsT=WGN[buf][:, ic * 128:(ic + 1) * 128],
                                        rhs=XO2[ic][:, n0:n0 + 512],
                                        start=(ic == 0), stop=(ic == DC - 1))
                            for ic in range(DC):
                                for n0 in (0, 512):
                                    ins = tn.matmul(
                                        out=PU1[:, n0:n0 + 512],
                                        lhsT=WUN[buf][:, ic * 128:(ic + 1) * 128],
                                        rhs=XO2[ic][:, n0:n0 + 512],
                                        start=(ic == 0), stop=(ic == DC - 1))
                            ctx.inc(ins, "f1mm")
                        gu_after[nt] = ctx.cnt["f1mm"]

                        @block.scalar
                        def _(a, nt=nt):
                            ctx.wait(a, "f1mm")
                            ctx.wait(a, "f1v")
                            ctx.inc(a.activation(out=SIL1[:, :], in_=PG1[:, :],
                                                 func=AF.Silu,
                                                 scale=1.0 / 127.0), "f1a")

                        @block.vector
                        def _(v, nt=nt):
                            ctx.wait(v, "f1a")
                            ctx.inc(v.scalar_tensor_tensor(
                                out=HT1[nt][:, :], in0=PU1[:, :],
                                scalar=1.0 / 16129.0, in1=SIL1[:, :],
                                op0=ALU.mult, op1=ALU.mult), "f1v")
                    emit_barrier(ctx, block)

                with ExitStack() as pd:
                    cd = sub_ctx(ctx, pd)
                    WDN = [cd.sb(f"WDN{th}_{i}", [128, D], BF16)
                           for i in range(IC1)]
                    YF1 = [cd.sb(f"YF1_{th}_{i}", [128, D], BF16)
                           for i in range(2)]
                    PY1 = cd.ps(f"PY1_{th}", [128, D])

                    @block.gpsimd
                    def _(g, th=th):
                        for hc in range(IC1):
                            ctx.inc(g.dma_start(
                                out=WDN[hc][:, :],
                                in_=io["wdT"][hc * 128:(hc + 1) * 128, :]),
                                "d1wl", 16)

                    yst_after = {}
                    for ti in range(8):
                        tt = th * 8 + ti
                        buf = ti % 2

                        @block.tensor
                        def _(tn, ti=ti):
                            ctx.wait(tn, "d1wl")
                            ctx.wait(tn, "d1v")
                            for hc in range(IC1):
                                for n0 in (0, 512):
                                    ins = tn.matmul(
                                        out=PY1[:, n0:n0 + 512],
                                        lhsT=HT1[hc][:, ti * 128:(ti + 1) * 128],
                                        rhs=WDN[hc][:, n0:n0 + 512],
                                        start=(hc == 0), stop=(hc == IC1 - 1))
                            ctx.inc(ins, "d1mm")

                        @block.vector
                        def _(v, ti=ti, buf=buf):
                            ctx.wait(v, "d1mm")
                            if ti >= 2:
                                ctx.wait(v, "d1yst", yst_after[ti - 2])
                            ctx.inc(v.tensor_copy(out=YF1[buf][:, :],
                                                  in_=PY1[:, :]), "d1v")

                        @block.gpsimd
                        def _(g, tt=tt, buf=buf):
                            ctx.wait(g, "d1v")
                            ctx.inc(g.dma_start(
                                out=io["yffn_park"][tt * 128:(tt + 1) * 128, :],
                                in_=YF1[buf][:, :]), "d1yst", 16)
                        yst_after[ti] = ctx.cnt["d1yst"]
                    emit_barrier(ctx, block)

        emit_attn_pipelined(ctx, block, dict(
            mode="l1", tag="A1", ATTNT=ATTNT, EYE32=EYE32,
            keys_lhsT=lambda h, kc: K1T[(h // 4) // 2][
                64 * ((h // 4) % 2):64 * ((h // 4) % 2) + 64,
                kc * 128:(kc + 1) * 128],
            v_lhsT=lambda kc, h: V1[kc][:, 65 * (h // 4):65 * (h // 4) + 65],
            exp_bias=lambda h, kc: ALIBI[:, h * ST + kc:h * ST + kc + 1],
            qb_base=lambda h: 64 * ((h // 4) % 2),
            q_sb=lambda h, qc: Q1T[h // 2][
                64 * (h % 2):64 * (h % 2) + 64, qc * 1024:(qc + 1) * 1024],
        ))

        with ExitStack() as p8s:
            c8 = sub_ctx(ctx, p8s)
            WOSB = [c8.sb(f"WOSB{i}", [128, D], BF16) for i in range(DC)]
            XTL = [c8.sb(f"L1XTL{i}", [128, D]) for i in range(2)]
            YFB = [c8.sb(f"L1YFB{i}", [128, D], BF16) for i in range(2)]
            X2TL = [c8.sb(f"X2TL{i}", [128, D]) for i in range(2)]
            PYA = c8.ps("L1PYA", [128, D])

            @block.gpsimd
            def _(g):
                for c in range(DC):
                    ctx.inc(g.dma_start(out=WOSB[c][:, :],
                                        in_=io["Wo"][c * 128:(c + 1) * 128, :]),
                            "o1ld", 16)

            x2_after, st_after = {}, {}
            for tt in range(TT):
                buf = tt % 2

                @block.sync
                def _(sync, tt=tt, buf=buf):
                    if tt >= 2:
                        ctx.wait(sync, "o1v", x2_after[tt - 2])
                    ctx.inc(sync.dma_start(
                        out=XTL[buf][:, :],
                        in_=io["x1_in"][tt * 128:(tt + 1) * 128, :]), "o1xl", 16)
                    ctx.inc(sync.dma_start(
                        out=YFB[buf][:, :],
                        in_=io["yffn_park"][tt * 128:(tt + 1) * 128, :]),
                        "o1xl", 16)

                @block.tensor
                def _(tn, tt=tt):
                    ctx.wait(tn, "o1ld")
                    ctx.wait(tn, "o1v")
                    for c in range(DC):
                        lhs = ATTNT[c][:, tt * 128:(tt + 1) * 128]
                        for n0 in (0, 512):
                            ins = tn.matmul(out=PYA[:, n0:n0 + 512], lhsT=lhs,
                                            rhs=WOSB[c][:, n0:n0 + 512],
                                            start=(c == 0), stop=(c == DC - 1))
                    ctx.inc(ins, "o1mm")

                @block.vector
                def _(v, tt=tt, buf=buf):
                    ctx.wait(v, "o1mm")
                    ctx.wait(v, "o1xl")
                    if tt >= 2:
                        ctx.wait(v, "o1st", st_after[tt - 2])
                    v.scalar_tensor_tensor(
                        out=X2TL[buf][:, :], in0=PYA[:, :], scalar=1.0,
                        in1=XTL[buf][:, :], op0=ALU.mult, op1=ALU.add)
                    ctx.inc(v.tensor_tensor(out=X2TL[buf][:, :],
                                            in0=X2TL[buf][:, :],
                                            in1=YFB[buf][:, :], op=ALU.add),
                            "o1v")
                x2_after[tt] = ctx.cnt["o1v"]

                @block.gpsimd
                def _(g, tt=tt, buf=buf):
                    ctx.wait(g, "o1v")
                    ctx.inc(g.dma_start(
                        out=io["x2_out"][tt * 128:(tt + 1) * 128, :],
                        in_=X2TL[buf][:, :]), "o1st", 16)
                st_after[tt] = ctx.cnt["o1st"]

            @block.gpsimd
            def _(g):
                ctx.wait(g, "o1st")


def build_fused(debug=False):
    nc = bass.Bass("TRN2", target_bir_lowering=False)
    io = {}
    for name, shape, dt, kind in [
        ("x_in", [T, D], F32, "ExternalInput"),
        ("wg_own", [I0, 128], F32, "ExternalInput"),
        ("wu_own", [I0, 128], F32, "ExternalInput"),
        ("wd_own", [128, I0], F32, "ExternalInput"),
        ("agwd_in", [I0, 128], BF16, "Internal"),
        ("agwd_out", [8 * I0, 128], BF16, "Internal"),
        ("agwg_in", [128, I0], BF16, "Internal"),
        ("agwg_out", [D, I0], BF16, "Internal"),
        ("agwu_in", [128, I0], BF16, "Internal"),
        ("agwu_out", [D, I0], BF16, "Internal"),
        ("wv_wT", [D, D], F32, "ExternalInput"),
        ("wv_b", [1, D], F32, "ExternalInput"),
        ("out_wT", [D, D], F32, "ExternalInput"),
        ("out_b", [1, D], F32, "ExternalInput"),
        ("watt", [NH, HD, HD], F32, "ExternalInput"),
        ("eye", [128, 128], F32, "ExternalInput"),
        ("Wq", [D, D], F32, "ExternalInput"),
        ("Wkv", [D, 512], F32, "ExternalInput"),
        ("Wo", [D, D], F32, "ExternalInput"),
        ("wgT", [I1, D], I32, "ExternalInput"),
        ("wuT", [I1, D], I32, "ExternalInput"),
        ("wdT", [I1, D], I32, "ExternalInput"),
        ("alibi", [128, NH * ST], F32, "ExternalInput"),
        ("x2_out", [T, D], F32, "ExternalOutput"),
        ("h_park", [T, I0], BF16, "Internal"),
        ("yffn_park", [T, D], BF16, "Internal"),
        ("yffn1_park", [T, D], BF16, "Internal"),
        ("x1_park", [T, D], F32, "Internal"),
        ("xq_park", [D, T], BF16, "Internal"),
        ("ag0_in", [D, T], BF16, "Internal"),
        ("ag0_out", [2 * D, T], BF16, "Internal"),
        ("ag1_in", [D, T], BF16, "Internal"),
        ("ag1_out", [2 * D, T], BF16, "Internal"),
    ] + ([("dbg_x1", [T, D], F32, "ExternalOutput"),
          ("dbg_yffn", [T, D], BF16, "ExternalOutput"),
          ("dbg_xq", [D, T], BF16, "ExternalOutput"),
          ("dbg_h", [T, I0], BF16, "ExternalOutput")] if debug else []):
        kw = ({"addr_space": "Shared"}
              if name in ("agwg_out", "agwu_out", "agwd_out") else {})
        io[name] = nc.dram_tensor(name, shape, dt, kind=kind, **kw).ap()

    PAIRS = [[0, 1], [2, 3], [4, 5], [6, 7]]

    with ExitStack() as stack:
        ctx = Ctx(nc, stack)
        block = stack.enter_context(nc.Block())

        with ExitStack() as s0:
            c0 = sub_ctx(ctx, s0)
            xnT0 = [c0.sb(f"f_xnT0_{c}", [128, T], BF16) for c in range(DC)]
            xqT0 = [c0.sb(f"f_xqT0_{c}", [128, T], BF16) for c in range(DC)]
            tp = emit_ln_phase(c0, block, io["x_in"], xnT0, xqT0, 0, "ln0")

            @block.gpsimd
            def _(g):
                ctx.wait(g, tp)
                ctx.wait(g, "ln0_tpq")
                for c in range(DC):
                    ctx.inc(g.dma_start(
                        out=io["ag0_in"][c * 128:(c + 1) * 128, :],
                        in_=xnT0[c][:, :]), "ag0st", 16)
                    ctx.inc(g.dma_start(
                        out=io["xq_park"][c * 128:(c + 1) * 128, :],
                        in_=xqT0[c][:, :]), "ag0st", 16)
                ctx.wait(g, "ag0st")
                ctx.inc(g.collective_compute(
                    "AllGather", ALU.bypass, replica_groups=PAIRS,
                    ins=[io["ag0_in"]], outs=[io["ag0_out"]]), "cc0")
            emit_barrier(ctx, block, engines=("sync", "vector", "scalar"),
                         exclude=("cc0",))

        io_l0 = dict(io)
        io_l0["xqT_own"] = io["xq_park"]
        io_l0["xnT_own"] = io["ag0_in"]
        io_l0["x1_out"] = io["x1_park"]
        io_l0["xnT_loads"] = lambda c: [
            (0, T, io["ag0_out"][c * 128:(c + 1) * 128, :]),
            (T, S, io["ag0_out"][D + c * 128:D + (c + 1) * 128, :]),
        ]
        emit_l0_ffn(ctx, block, io_l0)
        emit_l0_attn_section(ctx, block, io_l0)

        with ExitStack() as s1:
            c1 = sub_ctx(ctx, s1)
            xn1T0 = [c1.sb(f"f_xn1T_{c}", [128, T], BF16) for c in range(DC)]
            tp1 = emit_ln_phase(c1, block, io["x1_park"], xn1T0, None, 0, "ln1")

            @block.gpsimd
            def _(g):
                ctx.wait(g, tp1)
                for c in range(DC):
                    ctx.inc(g.dma_start(
                        out=io["ag1_in"][c * 128:(c + 1) * 128, :],
                        in_=xn1T0[c][:, :]), "ag1st", 16)
                ctx.wait(g, "ag1st")
                ctx.inc(g.collective_compute(
                    "AllGather", ALU.bypass, replica_groups=PAIRS,
                    ins=[io["ag1_in"]], outs=[io["ag1_out"]]), "cc1")
            emit_barrier(ctx, block, engines=("sync", "vector", "scalar"),
                         exclude=("cc1",))

        io_l1 = dict(io)
        io_l1["x1_in"] = io["x1_park"]
        io_l1["xn1T_own"] = io["ag1_in"]
        io_l1["yffn_park"] = io["yffn1_park"]
        io_l1["xn1T_loads"] = lambda c: [
            (0, T, io["ag1_out"][c * 128:(c + 1) * 128, :]),
            (T, S, io["ag1_out"][D + c * 128:D + (c + 1) * 128, :]),
        ]
        emit_l1_body(ctx, stack, block, io_l1)
        if debug:
            @block.gpsimd
            def _(g):
                ctx.wait(g, "o1st")
                ctx.inc(g.dma_start(out=io["dbg_x1"], in_=io["x1_park"]), "dbgs", 16)
                ctx.inc(g.dma_start(out=io["dbg_yffn"], in_=io["yffn_park"]), "dbgs", 16)
                ctx.inc(g.dma_start(out=io["dbg_xq"], in_=io["xq_park"]), "dbgs", 16)
                ctx.inc(g.dma_start(out=io["dbg_h"], in_=io["h_park"]), "dbgs", 16)
                ctx.wait(g, "dbgs")
    return nc


def kernel_fused(L0_attn_x, L0_attn_w_att_val, L0_attn_wv_w, L0_attn_wv_b,
                 L0_attn_out_w, L0_attn_out_b, L0_ffn_w_gate, L0_ffn_w_up,
                 L0_ffn_w_down, L1_attn_Wq, L1_attn_Wkv, L1_attn_Wo,
                 L1_ffn_w_gate, L1_ffn_w_up, L1_ffn_w_down):
    x = np.ascontiguousarray(np.asarray(L0_attn_x, dtype=np.float32))
    cores = list(range(8))
    wv_wT = np.ascontiguousarray(np.asarray(L0_attn_wv_w, dtype=np.float32).T)
    out_wT = np.ascontiguousarray(np.asarray(L0_attn_out_w, dtype=np.float32).T)
    wv_b = np.ascontiguousarray(np.asarray(L0_attn_wv_b,
                                           dtype=np.float32).reshape(1, D))
    out_b = np.ascontiguousarray(np.asarray(L0_attn_out_b,
                                            dtype=np.float32).reshape(1, D))
    watt = np.ascontiguousarray(np.asarray(L0_attn_w_att_val, dtype=np.float32))
    w_gate = np.ascontiguousarray(np.asarray(L0_ffn_w_gate, dtype=np.float32))
    w_up = np.ascontiguousarray(np.asarray(L0_ffn_w_up, dtype=np.float32))
    w_down = np.ascontiguousarray(np.asarray(L0_ffn_w_down, dtype=np.float32))
    Wq = np.ascontiguousarray(np.asarray(L1_attn_Wq, dtype=np.float32))
    Wkv = np.ascontiguousarray(np.asarray(L1_attn_Wkv, dtype=np.float32))
    Wo = np.ascontiguousarray(np.asarray(L1_attn_Wo, dtype=np.float32))
    def _lhsT_shuffle(w):
        a = np.asarray(w, dtype=np.int32).reshape(I1 // 128, 128, DC, 128)
        return np.ascontiguousarray(a.transpose(0, 3, 2, 1).reshape(I1, D))
    wgT = _lhsT_shuffle(L1_ffn_w_gate)
    wuT = _lhsT_shuffle(L1_ffn_w_up)
    wdT = np.ascontiguousarray(np.asarray(L1_ffn_w_down, dtype=np.int32).T)
    alibi = make_alibi_table()
    eye = np.eye(128, dtype=np.float32)

    nc = build_fused()
    maps = []
    for c in cores:
        bi, hf = c // 2, c % 2
        wg_own = w_gate[:, c * 128:(c + 1) * 128]
        wu_own = w_up[:, c * 128:(c + 1) * 128]
        wd_own = w_down[c * 128:(c + 1) * 128, :]
        maps.append({
            "x_in": np.ascontiguousarray(x[bi, hf * T:(hf + 1) * T, :]),
            "wg_own": np.ascontiguousarray(wg_own),
            "wu_own": np.ascontiguousarray(wu_own),
            "wd_own": np.ascontiguousarray(wd_own),
            "wv_wT": wv_wT, "wv_b": wv_b, "out_wT": out_wT, "out_b": out_b,
            "watt": watt, "eye": eye,
            "Wq": Wq, "Wkv": Wkv, "Wo": Wo,
            "wgT": wgT, "wuT": wuT, "wdT": wdT, "alibi": alibi,
        })
    res = run_bass_kernel_spmd(nc, maps, core_ids=cores).results
    out = np.empty((4, S, D), dtype=np.float32)
    for c in cores:
        bi, hf = c // 2, c % 2
        out[bi, hf * T:(hf + 1) * T, :] = np.asarray(res[c]["x2_out"])
    return out


kernel = kernel_fused


def build_attn_test(**flags):
    nc = bass.Bass("TRN2", target_bir_lowering=False)
    io = {}
    for name, shape, dt, kind in [
        ("k1t_in", [2 * 128, S], BF16, "ExternalInput"),
        ("v1_in", [ST * 128, NKV * 65], BF16, "ExternalInput"),
        ("q1t_in", [D, T], BF16, "ExternalInput"),
        ("alibi", [128, NH * ST], F32, "ExternalInput"),
        ("eye", [128, 128], F32, "ExternalInput"),
        ("attnt_out", [D, T], BF16, "ExternalOutput"),
    ]:
        io[name] = nc.dram_tensor(name, shape, dt, kind=kind).ap()
    with ExitStack() as stack:
        ctx = Ctx(nc, stack)
        block = stack.enter_context(nc.Block())
        K1T = [ctx.sb(f"K1T{i}", [128, S], BF16) for i in range(2)]
        V1 = [ctx.sb(f"V1_{i}", [128, NKV * 65], BF16) for i in range(ST)]
        Q1T = [ctx.sb(f"Q1T{i}", [128, T], BF16) for i in range(DC)]
        ATTNT = [ctx.sb(f"ATTNT{c}", [128, T], BF16) for c in range(DC)]
        EYE32 = ctx.sb("EYE32", [128, 128])
        ALIBI = ctx.sb("ALIBI", [128, NH * ST])

        @block.sync
        def _(sync):
            for i in range(2):
                ctx.inc(sync.dma_start(out=K1T[i][:, :],
                                       in_=io["k1t_in"][i * 128:(i + 1) * 128, :]),
                        "ld", 16)
            for i in range(ST):
                ctx.inc(sync.dma_start(out=V1[i][:, :],
                                       in_=io["v1_in"][i * 128:(i + 1) * 128, :]),
                        "ld", 16)
            for c in range(DC):
                ctx.inc(sync.dma_start(out=Q1T[c][:, :],
                                       in_=io["q1t_in"][c * 128:(c + 1) * 128, :]),
                        "ld", 16)
            ctx.inc(sync.dma_start(out=EYE32[:, :], in_=io["eye"]), "ld", 16)
            ctx.inc(sync.dma_start(out=ALIBI[:, :], in_=io["alibi"]), "ld", 16)
        emit_barrier(ctx, block)

        emit_attn_pipelined(ctx, block, dict(
            mode="l1", tag="A1", ATTNT=ATTNT, EYE32=EYE32,
            keys_lhsT=lambda h, kc: K1T[(h // 4) // 2][
                64 * ((h // 4) % 2):64 * ((h // 4) % 2) + 64,
                kc * 128:(kc + 1) * 128],
            v_lhsT=lambda kc, h: V1[kc][:, 65 * (h // 4):65 * (h // 4) + 65],
            exp_bias=lambda h, kc: ALIBI[:, h * ST + kc:h * ST + kc + 1],
            qb_base=lambda h: 64 * ((h // 4) % 2),
            q_sb=lambda h, qc: Q1T[h // 2][
                64 * (h % 2):64 * (h % 2) + 64, qc * 1024:(qc + 1) * 1024],
            **flags,
        ))

        @block.gpsimd
        def _(g):
            for k, v in ctx.cnt.items():
                if v > 0:
                    g.wait_ge(ctx.sem(k), v)
            for c in range(DC):
                ctx.inc(g.dma_start(out=io["attnt_out"][c * 128:(c + 1) * 128, :],
                                    in_=ATTNT[c][:, :]), "stf", 16)
            ctx.wait(g, "stf")
    return nc
